# version 04
# speedup vs baseline: 1.3795x; 1.1784x over previous
import numpy as np
import ml_dtypes
from contextlib import ExitStack

import concourse.bass as bass
import concourse.mybir as mybir
from concourse.bass_utils import run_bass_kernel_spmd

F32 = mybir.dt.float32
BF16 = mybir.dt.bfloat16
I32 = mybir.dt.int32
AF = mybir.ActivationFunctionType
ALU = mybir.AluOpType
AX = mybir.AxisListType

NCORES = 8
B, S, D = 16, 2048, 1024
BPC = B // NCORES
T = BPC * S
NT = T // 128
ND = D // 128
ENGINES = ("sync", "gpsimd", "tensor", "vector", "scalar")

LAYER_NRV = {0: 3, 1: 0, 2: 8, 3: 8}


class Sem:
    def __init__(self, pb, name):
        self.h = pb.es.enter_context(pb.nc.semaphore(name))
        self.n = 0

    def inc(self, instr, k=1):
        instr.then_inc(self.h, k)
        self.n += k
        return self.n


class PB:

    def __init__(self, dbg=None):
        self.dbg = dbg or {}
        self.es = ExitStack()
        self.nc = bass.Bass("TRN2", target_bir_lowering=False)
        self.dram = {}
        self.sbuf = {}
        self.psum = {}
        self.sems = {}
        self.bar_n = 0

    def dram_t(self, name, shape, dtype, kind="Internal"):
        k = self.dbg.get(name, kind)
        k = {"in": "ExternalInput", "out": "ExternalOutput"}.get(k, k)
        t = self.nc.dram_tensor(name, shape, dtype, kind=k)
        self.dram[name] = t.ap()
        if not hasattr(self, "kinds"):
            self.kinds = {}
        self.kinds[name] = k
        return self.dram[name]

    def sb(self, name, shape, dtype):
        t = self.es.enter_context(self.nc.sbuf_tensor(name, shape, dtype))
        self.sbuf[name] = t
        return t

    def ps(self, name, shape, dtype):
        t = self.es.enter_context(self.nc.psum_tensor(name, shape, dtype))
        self.psum[name] = t
        return t

    SEM_GLOBALS = ("bar", "dumi", "c_init", "dbg")

    def sem(self, name):
        if name in self.sems:
            return self.sems[name]
        if name in self.SEM_GLOBALS or name.startswith("self_"):
            alias = name
        else:
            pref = name[:2]
            if not hasattr(self, "_ph_slots"):
                self._ph_slots = {}
                self._slot_pool = {}
            slots = self._ph_slots.setdefault(pref, {})
            if name not in slots:
                slots[name] = len(slots)
            alias = f"slot{slots[name]}"
        if alias not in self._slot_objs if hasattr(self, "_slot_objs") else True:
            pass
        if not hasattr(self, "_slot_objs"):
            self._slot_objs = {}
        if alias not in self._slot_objs:
            self._slot_objs[alias] = Sem(self, alias)
        self.sems[name] = self._slot_objs[alias]
        return self.sems[name]

    def start(self):
        self.block = self.es.enter_context(self.nc.Block())
        self.bar = self.sem("bar")
        self.dumi = self.sem("dumi")

        @self.sec("gpsimd")
        def _(g):
            self.dumi.inc(g.memset(self.sbuf["dum"][:, 3:4], 0.0), 1)

    def sec(self, engine):
        return getattr(self.block, engine)

    def barrier(self):
        target = self.bar_n + len(ENGINES)

        @self.sec("vector")
        def _(v):
            v.memset(self.sbuf["dum"][:, 0:1], 0.0).then_inc(self.bar.h, 1)
            v.wait_ge(self.bar.h, target)

        @self.sec("scalar")
        def _(sc):
            sc.wait_ge(self.dumi.h, 1)
            sc.activation(out=self.sbuf["dum"][:, 1:2], in_=self.sbuf["dum"][:, 3:4],
                          func=AF.Identity).then_inc(self.bar.h, 1)
            sc.wait_ge(self.bar.h, target)

        @self.sec("gpsimd")
        def _(g):
            g.memset(self.sbuf["dum"][:, 2:3], 0.0).then_inc(self.bar.h, 1)
            g.wait_ge(self.bar.h, target)

        for e in ("tensor", "sync"):
            @self.sec(e)
            def _(eng):
                eng.sem_inc(self.bar.h, 1)
                eng.wait_ge(self.bar.h, target)
        self.bar_n = target

    def vsync(self, v, instr, tag="v"):
        s = self.sem("self_" + tag)
        s.inc(instr, 1)
        v.wait_ge(s.h, s.n)

    def finish(self):
        self.es.close()


def alloc_common(pb):
    pb.dram_t("xcur", [T, D], F32)
    pb.dram_t("xcur2", [T, D], F32)
    pb.dram_t("xnT", [D, T], BF16)
    pb.dram_t("acc_a", [T, D], F32)
    pb.dram_t("acc_f", [T, D], F32)
    pb.dram_t("rv", [8, T], F32)
    pb.dram_t("rmask", [8, T], F32)
    pb.dram_t("rmcol", [128, NT * 8], F32)
    pb.dram_t("acc_aT", [D, T], BF16)
    pb.dram_t("xl_tok", [T, D], BF16)
    pb.sb("xf0", [128, D], F32)
    pb.sb("xf1", [128, D], F32)
    pb.sb("st0", [128, 16], F32)
    pb.sb("st1", [128, 16], F32)
    pb.sb("ssq0", [128, 1], F32)
    pb.sb("ssq1", [128, 1], F32)
    pb.sb("rvecs", [128, D * 8], F32)
    pb.sb("rsum", [128, 8], F32)
    pb.sb("scores", [128, NT * 8], F32)
    pb.sb("epsc", [128, 1], F32)
    pb.sb("ln32c", [128, 1], F32)
    pb.sb("dum", [128, 4], F32)
    pb.sb("ident", [128, 128], BF16)
    pb.sb("onesb", [128, 128], BF16)
    for k in range(ND):
        pb.sb(f"xnT{k}", [128, T], BF16)
    pb.sb("wB", [128, 8192], BF16)
    pb.sb("mvA0", [128, 4096], BF16)
    pb.sb("mvA1", [128, 4096], BF16)
    pb.sb("hT", [128, 16640], BF16)
    pb.sb("silg", [128, 1024], BF16)
    pb.sb("pt0", [128, 2048], BF16)
    pb.sb("pt1", [128, 2048], BF16)
    pb.sb("gen0", [128, 2048], F32)
    pb.sb("gen1", [128, 2048], F32)
    pb.sb("ob0", [128, 1024], F32)
    pb.sb("ob1", [128, 1024], F32)
    pb.sb("small", [128, 512], F32)
    pb.sb("carry", [128, 8], F32)
    for i in range(6):
        pb.ps(f"pf{i}", [128, 512], F32)
    pb.ps("pb0", [128, 1024], BF16)
    pb.ps("pb1", [128, 1024], BF16)


def init_consts(pb):
    s_c = pb.sem("c_init")

    @pb.sec("gpsimd")
    def _(g):
        s_c.inc(g.dma_start(out=pb.sbuf["ident"][:], in_=pb.dram["identity"][:]), 16)
        s_c.inc(g.dma_start(out=pb.sbuf["onesb"][:], in_=pb.dram["onesf"][:]), 16)

    @pb.sec("vector")
    def _(v):
        v.memset(pb.sbuf["epsc"][:], 1e-5)
        v.memset(pb.sbuf["ln32c"][:], float(np.log(32.0)))
        v.memset(pb.sbuf["small"][:], 1.0)
        v.wait_ge(s_c.h, s_c.n)
    pb.barrier()



P0_XIN = {0: "x0", 1: "x0", 2: "xcur", 3: "xcur2"}
P0_XOUT = {1: "xcur", 2: "xcur2", 3: "xcur"}


def p0_ln(pb, layer, fold=None, acc_in=("acc_a", "acc_f")):
    nc = pb.nc
    first = layer == 0
    fold = (not first) if fold is None else fold
    last = layer == 3
    xin = pb.dram[P0_XIN[layer]]
    nrv = LAYER_NRV[layer]

    sb = pb.sbuf
    xf = [sb["xf0"], sb["xf1"]]
    af = [sb["gen0"][:, 0:D], sb["gen1"][:, 0:D]]
    ff = [sb["gen0"][:, D:2 * D], sb["gen1"][:, D:2 * D]]
    xnb = [sb["pt0"][:, 0:D], sb["pt1"][:, 0:D]]
    xlb = [sb["pt0"][:, D:2 * D], sb["pt1"][:, D:2 * D]]
    st = [sb["st0"], sb["st1"]]
    ssq = [sb["ssq0"], sb["ssq1"]]
    xnT = [sb[f"xnT{k}"] for k in range(ND)]
    ptr = [pb.psum["pb0"], pb.psum["pb1"]]

    s_xld = [pb.sem("p0x_0"), pb.sem("p0x_1")]
    s_ald = [pb.sem("p0a_0"), pb.sem("p0a_1")]
    s_v = pb.sem("p0v")
    s_a = pb.sem("p0s")
    s_t = pb.sem("p0t")
    s_rv = pb.sem("p0r")
    s_out = pb.sem("p0o")
    s_xl = [pb.sem("p0xl_0"), pb.sem("p0xl_1")]

    V_PER = (4 if fold else 3) + (2 if first else 0)
    A_PER = 4 if first else 2
    v0, a0, t0 = s_v.n, s_a.n, s_t.n
    x0_ = [s_xld[0].n, s_xld[1].n]
    al0 = [s_ald[0].n, s_ald[1].n]

    def xld_t(i):
        return x0_[i % 2] + 16 * (i // 2 + 1)

    def ald_t(i):
        return al0[i % 2] + 32 * (i // 2 + 1)

    def v_resadd(i):
        return v0 + V_PER * i + 1

    def v_var(i):
        return v0 + V_PER * i + (2 if fold else 1)

    def v_xn(i):
        return v0 + V_PER * i + (3 if fold else 2)

    def v_xl(i):
        return v0 + V_PER * i + (3 if fold else 2) + 1

    def v_evac(i):
        return v0 + V_PER * i + V_PER

    def a_ssq(i):
        return a0 + A_PER * i + 1

    def a_rstd(i):
        return a0 + A_PER * i + 2

    def t_tr(i):
        return t0 + i + 1

    @pb.sec("gpsimd")
    def _(g):
        if nrv:
            s_rv.inc(g.dma_start(out=sb["rvecs"][:, 0:nrv * D], in_=pb.dram[f"rvl{layer}"][:]), 16)
        if fold:
            for i in range(NT):
                if i >= 2:
                    g.wait_ge(s_v.h, v_resadd(i - 2))
                g.dma_start(out=af[i % 2], in_=pb.dram[acc_in[0]][i * 128:(i + 1) * 128, :]).then_inc(s_ald[i % 2].h, 16)
                g.dma_start(out=ff[i % 2], in_=pb.dram[acc_in[1]][i * 128:(i + 1) * 128, :]).then_inc(s_ald[i % 2].h, 16)
                s_ald[i % 2].n += 32

    out_t = {}
    xlst_t = {}

    @pb.sec("sync")
    def _(s):
        for i in range(NT):
            if i >= 2:
                s.wait_ge(s_v.h, v_xn(i - 2))
                s.wait_ge(s_a.h, a_ssq(i - 2))
            if fold and i >= 2:
                s.wait_ge(s_out.h, out_t[i - 2])
            s_xld[i % 2].inc(s.dma_start(out=xf[i % 2][:], in_=xin[i * 128:(i + 1) * 128, :]), 16)
            if first:
                s.wait_ge(s_v.h, v_xl(i) + 1)
                xlst_t[i] = s_xl[i % 2].inc(s.dma_start(out=pb.dram["xl_tok"][i * 128:(i + 1) * 128, :],
                                                        in_=xlb[i % 2]), 16)
            if fold:
                s.wait_ge(s_v.h, v_resadd(i))
                dst = pb.dram[P0_XOUT[layer]]
                out_t[i] = s_out.inc(s.dma_start(out=dst[i * 128:(i + 1) * 128, :], in_=xf[i % 2][:]), 16)
        if first:
            s.wait_ge(s_xl[0].h, s_xl[0].n)
            s.wait_ge(s_xl[1].h, s_xl[1].n)
        s.wait_ge(s_v.h, v0 + V_PER * NT)
        for k in range(ND):
            s_out.inc(s.dma_start(out=pb.dram["xnT"][k * 128:(k + 1) * 128, :], in_=xnT[k][:]), 16)
        if nrv:
            for r in range(nrv + (1 if first else 0)):
                src = sb["scores"][:].rearrange("p (i r) -> p r i", r=8)[:, r]
                dst = pb.dram["rv"][r:r + 1, :].rearrange("o (i p) -> (o p) i", p=128)
                s_out.inc(s.dma_start(out=dst, in_=src, allow_slow_non_contiguous=True), 16)
        s.wait_ge(s_out.h, s_out.n)

    @pb.sec("scalar")
    def _(sc):
        for i in range(NT):
            b = i % 2
            sc.wait_ge(s_xld[b].h, xld_t(i))
            if fold:
                sc.wait_ge(s_v.h, v_resadd(i))
            s_a.inc(sc.activation(out=sb["ob1"][:], in_=xf[b][:], func=AF.Square,
                                  accum_out=ssq[b][:, 0:1]), 1)
            sc.wait_ge(s_v.h, v_var(i))
            pb.vsync(sc, sc.activation(out=st[b][:, 5:6], in_=st[b][:, 3:4], func=AF.Ln,
                                       bias=sb["epsc"][:, 0:1]), tag="a")
            s_a.inc(sc.activation(out=st[b][:, 4:5], in_=st[b][:, 5:6], func=AF.Exp,
                                  scale=-0.5), 1)
            if first:
                sc.wait_ge(s_v.h, v_xn(i))
                pb.vsync(sc, sc.activation(out=sb["ob1"][:], in_=xnb[b][:], func=AF.Square,
                                           accum_out=st[b][:, 10:11]), tag="a")
                pb.vsync(sc, sc.activation(out=st[b][:, 11:12], in_=st[b][:, 10:11],
                                           func=AF.Ln), tag="a")
                s_a.inc(sc.activation(out=sb["scores"][:, i * 8 + 3:i * 8 + 4],
                                      in_=st[b][:, 11:12], func=AF.Exp, scale=-0.5,
                                      bias=sb["ln32c"][:, 0:1]), 1)
                sc.wait_ge(s_v.h, v_xl(i))
                pb.vsync(sc, sc.activation(out=st[b][:, 14:15], in_=st[b][:, 13:14],
                                           func=AF.Ln, bias=sb["epsc"][:, 0:1]), tag="a")
                s_a.inc(sc.activation(out=st[b][:, 14:15], in_=st[b][:, 14:15],
                                      func=AF.Exp, scale=-0.5), 1)

    @pb.sec("vector")
    def _(v):
        if nrv:
            v.wait_ge(s_rv.h, s_rv.n)
            for r in range(nrv):
                v.reduce_sum(sb["rsum"][:, r:r + 1], sb["rvecs"][:, r * D:(r + 1) * D], axis=AX.X)
        for i in range(NT):
            b = i % 2
            v.wait_ge(s_xld[i % 2].h, xld_t(i))
            if fold:
                v.wait_ge(s_ald[i % 2].h, ald_t(i))
                pb.vsync(v, v.tensor_tensor(out=xf[b][:], in0=xf[b][:], in1=af[b], op=ALU.add))
                s_v.inc(v.tensor_tensor(out=xf[b][:], in0=xf[b][:], in1=ff[b], op=ALU.add), 1)
            pb.vsync(v, v.reduce_sum(st[b][:, 0:1], xf[b][:], axis=AX.X))
            pb.vsync(v, v.tensor_scalar(out=st[b][:, 1:2], in0=st[b][:, 0:1], scalar1=1.0 / D,
                                        scalar2=None, op0=ALU.mult))
            pb.vsync(v, v.tensor_tensor(out=st[b][:, 2:3], in0=st[b][:, 1:2], in1=st[b][:, 1:2], op=ALU.mult))
            v.wait_ge(s_a.h, a_ssq(i))
            s_v.inc(v.scalar_tensor_tensor(out=st[b][:, 3:4], in0=ssq[b][:, 0:1],
                                           scalar=1.0 / D, in1=st[b][:, 2:3],
                                           op0=ALU.mult, op1=ALU.subtract), 1)
            v.wait_ge(s_a.h, a_rstd(i))
            if i >= 2:
                v.wait_ge(s_t.h, t_tr(i - 2))
            ins_xn = v.tensor_scalar(out=xnb[b][:], in0=xf[b][:], scalar1=st[b][:, 1:2],
                                     scalar2=st[b][:, 4:5], op0=ALU.subtract, op1=ALU.mult)
            for r in range(nrv):
                pb.vsync(v, v.tensor_tensor(out=sb["ob0"][:], in0=xf[b][:],
                                             in1=sb["rvecs"][:, r * D:(r + 1) * D], op=ALU.mult))
                pb.vsync(v, v.reduce_sum(st[b][:, 6:7], sb["ob0"][:], axis=AX.X))
                pb.vsync(v, v.tensor_scalar(out=st[b][:, 7:8], in0=st[b][:, 1:2],
                                            scalar1=sb["rsum"][:, r:r + 1], scalar2=None, op0=ALU.mult))
                pb.vsync(v, v.tensor_tensor(out=st[b][:, 8:9], in0=st[b][:, 6:7], in1=st[b][:, 7:8],
                                            op=ALU.subtract))
                ins_xn = v.tensor_tensor(out=sb["scores"][:, i * 8 + r:i * 8 + r + 1],
                                         in0=st[b][:, 8:9], in1=st[b][:, 4:5], op=ALU.mult)
            s_v.inc(ins_xn, 1)
            if first:
                pb.vsync(v, v.reduce_sum(st[b][:, 12:13], xnb[b][:], axis=AX.X))
                pb.vsync(v, v.tensor_scalar(out=st[b][:, 12:13], in0=st[b][:, 12:13],
                                            scalar1=1.0 / D, scalar2=None, op0=ALU.mult))
                pb.vsync(v, v.tensor_tensor(out=st[b][:, 15:16], in0=st[b][:, 12:13],
                                            in1=st[b][:, 12:13], op=ALU.mult))
                v.wait_ge(s_a.h, a0 + A_PER * i + 3)
                s_v.inc(v.scalar_tensor_tensor(out=st[b][:, 13:14], in0=st[b][:, 10:11],
                                               scalar=1.0 / D, in1=st[b][:, 15:16],
                                               op0=ALU.mult, op1=ALU.subtract), 1)
                v.wait_ge(s_a.h, a0 + A_PER * i + 4)
                if i >= 2:
                    v.wait_ge(s_xl[i % 2].h, xlst_t[i - 2])
                s_v.inc(v.tensor_scalar(out=xlb[b], in0=xnb[b][:], scalar1=st[b][:, 12:13],
                                        scalar2=st[b][:, 14:15], op0=ALU.subtract,
                                        op1=ALU.mult), 1)
            v.wait_ge(s_t.h, t_tr(i))
            for k in range(ND):
                last_k = k == ND - 1
                ins = v.tensor_copy(xnT[k][:, i * 128:(i + 1) * 128], ptr[b][:, k * 128:(k + 1) * 128])
                if last_k:
                    s_v.inc(ins, 1)

    @pb.sec("tensor")
    def _(t):
        for i in range(NT):
            b = i % 2
            t.wait_ge(s_v.h, v_xn(i))
            if i >= 2:
                t.wait_ge(s_v.h, v_evac(i - 2))
            for k in range(ND):
                ins = nc.tensor.transpose(ptr[b][:, k * 128:(k + 1) * 128],
                                          xnb[b][:, k * 128:(k + 1) * 128], sb["ident"][:])
                if k == ND - 1:
                    s_t.inc(ins, 1)

    pb.barrier()



def p_ffn1(pb):
    nc = pb.nc
    sb = pb.sbuf
    CH = 512
    NCH = T // CH
    M1 = 24
    xnT = [sb[f"xnT{k}"] for k in range(ND)]
    wg_sl = [sb["wB"][:, 0:1024], sb["wB"][:, 2048:3072]]
    wu_sl = [sb["wB"][:, 1024:2048], sb["wB"][:, 3072:4096]]
    wd_sl = [sb["wB"][:, 4096:4608], sb["wB"][:, 4608:5120]]
    silg = [sb["silg"][:, 0:512], sb["silg"][:, 512:1024]]
    stage = [sb["ob0"][:], sb["ob1"][:], sb["gen0"][:, 0:1024], sb["gen1"][:, 0:1024]]
    pg, pu = pb.psum["pf0"], pb.psum["pf1"]
    po = [pb.psum[f"pf{i}"] for i in range(2, 6)]

    s_w1 = [pb.sem("f1w_0"), pb.sem("f1w_1")]
    s_wd = [pb.sem("f1d_0"), pb.sem("f1d_1")]
    s_mm = pb.sem("f1mm")
    s_m2 = pb.sem("f1m2")
    s_sg = pb.sem("f1sg")
    s_h = pb.sem("f1h")
    s_e = pb.sem("f1e")
    s_st = pb.sem("f1st")

    w1_t, wd_t, mmg_t, mmu_t, m2_t, sg_t, h_t, e_t, st_t = {}, {}, {}, {}, {}, {}, {}, {}, {}
    cw1 = [s_w1[0].n, s_w1[1].n]
    cwd = [s_wd[0].n, s_wd[1].n]
    cmm, cm2, csg, ch_, ce, cst = s_mm.n, s_m2.n, s_sg.n, s_h.n, s_e.n, s_st.n
    for c in range(NCH):
        for m in range(M1):
            pm = m % 2
            cw1[pm] += 32
            w1_t[(c, m)] = cw1[pm]
            cmm += 1
            mmg_t[(c, m)] = cmm
            cmm += 1
            mmu_t[(c, m)] = cmm
            csg += 1
            sg_t[(c, m)] = csg
            ch_ += 1
            h_t[(c, m)] = ch_
        for n in range(2):
            for k in range(M1):
                pk = k % 2
                cwd[pk] += 16
                wd_t[(c, n, k)] = cwd[pk]
                cm2 += 1
                m2_t[(c, n, k)] = cm2
            ce += 4
            e_t[(c, n)] = ce
        cst += 64
        st_t[c] = cst

    def prev_nk(n, k):
        return (n, k - 2) if k >= 2 else (n - 1, M1 - 2 + k)

    @pb.sec("gpsimd")
    def _(g):
        for c in range(NCH):
            for m in range(M1):
                pm = m % 2
                occ = (c, m - 2) if m >= 2 else (c - 1, M1 - 2 + m)
                if occ in mmu_t:
                    g.wait_ge(s_mm.h, mmu_t[occ])
                g.dma_start(out=wg_sl[pm],
                            in_=pb.dram["wg1t"][m].rearrange("p k c -> p (k c)")).then_inc(s_w1[pm].h, 16)
                g.dma_start(out=wu_sl[pm],
                            in_=pb.dram["wu1t"][m].rearrange("p k c -> p (k c)")).then_inc(s_w1[pm].h, 16)
                s_w1[pm].n += 32
            for n in range(2):
                for k in range(M1):
                    pk = k % 2
                    pn, pk2 = prev_nk(n, k)
                    occ = (c, pn, pk2) if pn >= 0 else ((c - 1, 1, pk2) if c > 0 else None)
                    if occ in m2_t:
                        g.wait_ge(s_m2.h, m2_t[occ])
                    s_wd[pk].inc(g.dma_start(out=wd_sl[pk],
                                             in_=pb.dram["wd1t"][k * 128:(k + 1) * 128,
                                                                 n * 512:(n + 1) * 512]), 16)

    @pb.sec("tensor")
    def _(t):
        for c in range(NCH):
            for m in range(M1):
                pm = m % 2
                t.wait_ge(s_w1[pm].h, w1_t[(c, m)])
                prv = (c, m - 1) if m >= 1 else (c - 1, M1 - 1)
                if prv in sg_t:
                    t.wait_ge(s_sg.h, sg_t[prv])
                    t.wait_ge(s_h.h, h_t[prv])
                for k in range(ND):
                    ins = nc.tensor.matmul(pg[:], wg_sl[pm][:, k * 128:(k + 1) * 128],
                                           xnT[k][:, c * CH:(c + 1) * CH],
                                           start=(k == 0), stop=(k == ND - 1))
                s_mm.inc(ins, 1)
                for k in range(ND):
                    ins = nc.tensor.matmul(pu[:], wu_sl[pm][:, k * 128:(k + 1) * 128],
                                           xnT[k][:, c * CH:(c + 1) * CH],
                                           start=(k == 0), stop=(k == ND - 1))
                s_mm.inc(ins, 1)
            t.wait_ge(s_h.h, h_t[(c, M1 - 1)])
            for n in range(2):
                if n == 1:
                    t.wait_ge(s_e.h, e_t[(c, 0)])
                for k in range(M1):
                    pk = k % 2
                    t.wait_ge(s_wd[pk].h, wd_t[(c, n, k)])
                    for mt in range(4):
                        ins = nc.tensor.matmul(po[mt][:],
                                               sb["hT"][:, k * CH + mt * 128:k * CH + (mt + 1) * 128],
                                               wd_sl[pk], start=(k == 0), stop=(k == M1 - 1))
                    s_m2.inc(ins, 1)

    @pb.sec("scalar")
    def _(sc):
        for c in range(NCH):
            for m in range(M1):
                pm = m % 2
                sc.wait_ge(s_mm.h, mmg_t[(c, m)])
                occ = (c, m - 2) if m >= 2 else (c - 1, M1 - 2 + m)
                if occ in h_t:
                    sc.wait_ge(s_h.h, h_t[occ])
                s_sg.inc(sc.activation(out=silg[pm], in_=pg[:], func=AF.Silu,
                                       scale=1.0 / 127.0), 1)

    @pb.sec("vector")
    def _(v):
        for c in range(NCH):
            for m in range(M1):
                pm = m % 2
                v.wait_ge(s_mm.h, mmu_t[(c, m)])
                v.wait_ge(s_sg.h, sg_t[(c, m)])
                if m == 0 and c > 0:
                    v.wait_ge(s_m2.h, m2_t[(c - 1, 1, M1 - 1)])
                s_h.inc(v.scalar_tensor_tensor(out=sb["hT"][:, m * CH:(m + 1) * CH],
                                               in0=pu[:], scalar=1.0 / 127.0, in1=silg[pm],
                                               op0=ALU.mult, op1=ALU.mult), 1)
            for n in range(2):
                v.wait_ge(s_m2.h, m2_t[(c, n, M1 - 1)])
                if n == 0 and c > 0:
                    v.wait_ge(s_st.h, st_t[c - 1])
                for mt in range(4):
                    s_e.inc(v.tensor_scalar(out=stage[mt][:, n * 512:(n + 1) * 512],
                                            in0=po[mt][:], scalar1=1.0 / 127.0,
                                            scalar2=None, op0=ALU.mult), 1)

    @pb.sec("sync")
    def _(s):
        for c in range(NCH):
            s.wait_ge(s_e.h, e_t[(c, 1)])
            for mt in range(4):
                row = (c * 4 + mt) * 128
                s_st.inc(s.dma_start(out=pb.dram["acc_f"][row:row + 128, :], in_=stage[mt]), 16)
        s.wait_ge(s_st.h, s_st.n)

    pb.barrier()



def p_attn3(pb):
    nc = pb.nc
    sb = pb.sbuf
    CH = 512
    NCH = T // CH
    CPS = S // CH
    xnT = [sb[f"xnT{k}"] for k in range(ND)]
    wsl = [sb["wB"][:, 0:4096], sb["wB"][:, 4096:8192]]
    qb = [sb["gen0"][:, 0:512], sb["gen1"][:, 0:512]]
    kb = [sb["gen0"][:, 512:1024], sb["gen1"][:, 512:1024]]
    vb = [sb["gen0"][:, 1024:1536], sb["gen1"][:, 1024:1536]]
    gb = [sb["gen0"][:, 1536:2048], sb["gen1"][:, 1536:2048]]
    kv = sb["ob1"][:, 0:512]
    kvc = sb["ob0"][:, 0:512]
    tmp = sb["ob1"][:, 512:1024]
    outT = [sb["mvA0"], sb["mvA1"]]
    pq, pk, pv, pg_ = (pb.psum[f"pf{i}"] for i in range(4))

    s_w = [pb.sem("a3w_0"), pb.sem("a3w_1")]
    s_b = pb.sem("a3b")
    s_mm = pb.sem("a3mm")
    s_a = pb.sem("a3a")
    s_v = pb.sem("a3v")
    s_st = [pb.sem("a3st_0"), pb.sem("a3st_1")]

    mm_t, a_t, v_t, st_t = {}, {}, {}, {}
    cmm, ca, cv = s_mm.n, s_a.n, s_v.n
    cst = [s_st[0].n, s_st[1].n]
    w_t = {}
    cw = [s_w[0].n, s_w[1].n]
    for j in range(ND):
        pj = j % 2
        cw[pj] += 64
        w_t[j] = cw[pj]
        for c in range(NCH):
            cmm += 4
            mm_t[(j, c)] = cmm
            ca += 4
            a_t[(j, c)] = ca
            cv += 1
            v_t[(j, c)] = cv
        cst[pj] += 16
        st_t[j] = cst[pj]

    @pb.sec("gpsimd")
    def _(g):
        s_b.inc(g.dma_start(out=sb["scores"][:, 0:24], in_=pb.dram["b3qkv"][:]), 16)
        s_b.inc(g.dma_start(out=sb["scores"][:, 24:32], in_=pb.dram["b3g"][:]), 16)
        for j in range(ND):
            pj = j % 2
            if j >= 2:
                g.wait_ge(s_mm.h, mm_t[(j - 2, NCH - 1)])
            for bi, m in enumerate([j, 8 + j, 16 + j]):
                g.dma_start(out=wsl[pj][:, bi * 1024:(bi + 1) * 1024],
                            in_=pb.dram["wqkv3t"][m].rearrange("p k c -> p (k c)")).then_inc(s_w[pj].h, 16)
            g.dma_start(out=wsl[pj][:, 3 * 1024:4 * 1024],
                        in_=pb.dram["wgate3t"][j].rearrange("p k c -> p (k c)")).then_inc(s_w[pj].h, 16)
            s_w[pj].n += 64

    @pb.sec("tensor")
    def _(t):
        for j in range(ND):
            pj = j % 2
            t.wait_ge(s_w[pj].h, w_t[j])
            for c in range(NCH):
                if (j, c) != (0, 0):
                    pv_ = (j, c - 1) if c >= 1 else (j - 1, NCH - 1)
                    t.wait_ge(s_a.h, a_t[pv_])
                for bi, dst in enumerate([pq, pk, pv, pg_]):
                    for k in range(ND):
                        ins = nc.tensor.matmul(dst[:], wsl[pj][:, bi * 1024 + k * 128:bi * 1024 + (k + 1) * 128],
                                               xnT[k][:, c * CH:(c + 1) * CH],
                                               start=(k == 0), stop=(k == ND - 1))
                    s_mm.inc(ins, 1)

    @pb.sec("scalar")
    def _(sc):
        for j in range(ND):
            for c in range(NCH):
                pc = c % 2
                sc.wait_ge(s_mm.h, mm_t[(j, c)])
                sc.wait_ge(s_b.h, s_b.n)
                if (j, c) not in ((0, 0), (0, 1)):
                    pv_ = (j, c - 2) if c >= 2 else (j - 1, NCH - 2 + c)
                    sc.wait_ge(s_v.h, v_t[pv_])
                sc.activation(out=qb[pc], in_=pq[:], func=AF.Identity, bias=sb["scores"][:, j:j + 1])
                sc.activation(out=kb[pc], in_=pk[:], func=AF.Identity, bias=sb["scores"][:, 8 + j:9 + j])
                sc.activation(out=vb[pc], in_=pv[:], func=AF.Identity, bias=sb["scores"][:, 16 + j:17 + j])
                ins = sc.activation(out=gb[pc], in_=pg_[:], func=AF.Sigmoid, bias=sb["scores"][:, 24 + j:25 + j])
                s_a.inc(ins, 4)

    @pb.sec("vector")
    def _(v):
        for j in range(ND):
            pj = j % 2
            for c in range(NCH):
                pc = c % 2
                v.wait_ge(s_a.h, a_t[(j, c)])
                if c == 0 and j >= 2:
                    v.wait_ge(s_st[pj].h, st_t[j - 2])
                pb.vsync(v, v.tensor_tensor(out=kv, in0=kb[pc], in1=vb[pc], op=ALU.mult))
                pb.vsync(v, v.tensor_tensor_scan(out=kvc, data0=sb["small"][:], data1=kv,
                                                 initial=0.0, op0=ALU.mult, op1=ALU.add))
                if c % CPS != 0:
                    pb.vsync(v, v.tensor_scalar(out=kvc, in0=kvc, scalar1=sb["carry"][:, j:j + 1],
                                                scalar2=None, op0=ALU.add))
                pb.vsync(v, v.tensor_copy(sb["carry"][:, j:j + 1], kvc[:, 511:512]))
                pb.vsync(v, v.tensor_tensor(out=tmp, in0=qb[pc], in1=kvc, op=ALU.mult))
                s_v.inc(v.tensor_tensor(out=outT[pj][:, c * CH:(c + 1) * CH], in0=tmp,
                                        in1=gb[pc], op=ALU.mult), 1)

    @pb.sec("sync")
    def _(s):
        for j in range(ND):
            pj = j % 2
            s.wait_ge(s_v.h, v_t[(j, NCH - 1)])
            s_st[pj].inc(s.dma_start(out=pb.dram["acc_aT"][j * 128:(j + 1) * 128, :], in_=outT[pj][:]), 16)
        s.wait_ge(s_st[0].h, s_st[0].n)
        s.wait_ge(s_st[1].h, s_st[1].n)

    pb.barrier()



def p_final(pb):
    nc = pb.nc
    sb = pb.sbuf
    xf = [sb["xf0"], sb["xf1"]]
    ff = [sb["gen0"][:, 0:D], sb["gen1"][:, 0:D]]
    at = [sb["pt0"][:, 0:D], sb["pt1"][:, 0:D]]
    ptr = [pb.psum["pb0"], pb.psum["pb1"]]

    s_x = [pb.sem("pfx_0"), pb.sem("pfx_1")]
    s_f = [pb.sem("pff_0"), pb.sem("pff_1")]
    s_aT = [pb.sem("pfa_0"), pb.sem("pfa_1")]
    s_t = pb.sem("pft")
    s_v = pb.sem("pfv")
    s_st = pb.sem("pfo")

    t0, v0 = s_t.n, s_v.n
    x0b = [s_x[0].n, s_x[1].n]
    f0b = [s_f[0].n, s_f[1].n]
    a0b = [s_aT[0].n, s_aT[1].n]
    st0b = s_st.n

    @pb.sec("sync")
    def _(s):
        for i in range(NT):
            b = i % 2
            if i >= 2:
                s.wait_ge(s_v.h, v0 + (i - 1))
                s.wait_ge(s_st.h, st0b + 16 * (i - 1))
            s.dma_start(out=xf[b][:], in_=pb.dram["xcur"][i * 128:(i + 1) * 128, :]).then_inc(s_x[b].h, 16)
            s_x[b].n += 16
            s.dma_start(out=ff[b], in_=pb.dram["acc_f"][i * 128:(i + 1) * 128, :]).then_inc(s_f[b].h, 16)
            s_f[b].n += 16
            s.wait_ge(s_v.h, v0 + i + 1)
            s_st.inc(s.dma_start(out=pb.dram["y"][i * 128:(i + 1) * 128, :], in_=xf[b][:]), 16)
        s.wait_ge(s_st.h, s_st.n)

    @pb.sec("gpsimd")
    def _(g):
        for i in range(NT):
            b = i % 2
            if i >= 2:
                g.wait_ge(s_t.h, t0 + (i - 1))
            for k in range(ND):
                g.dma_start(out=at[b][:, k * 128:(k + 1) * 128],
                            in_=pb.dram["acc_aT"][k * 128:(k + 1) * 128, i * 128:(i + 1) * 128]).then_inc(s_aT[b].h, 16)
            s_aT[b].n += 16 * ND

    @pb.sec("tensor")
    def _(t):
        for i in range(NT):
            b = i % 2
            t.wait_ge(s_aT[b].h, a0b[b] + (i // 2 + 1) * 16 * ND)
            if i >= 2:
                t.wait_ge(s_v.h, v0 + i - 1)
            for k in range(ND):
                ins = nc.tensor.transpose(ptr[b][:, k * 128:(k + 1) * 128],
                                          at[b][:, k * 128:(k + 1) * 128], sb["ident"][:])
            s_t.inc(ins, 1)

    @pb.sec("vector")
    def _(v):
        for i in range(NT):
            b = i % 2
            v.wait_ge(s_x[b].h, x0b[b] + (i // 2 + 1) * 16)
            v.wait_ge(s_f[b].h, f0b[b] + (i // 2 + 1) * 16)
            v.wait_ge(s_t.h, t0 + i + 1)
            pb.vsync(v, v.tensor_tensor(out=xf[b][:], in0=xf[b][:], in1=ff[b], op=ALU.add))
            s_v.inc(v.tensor_tensor(out=xf[b][:], in0=xf[b][:], in1=ptr[b][:], op=ALU.add), 1)

    pb.barrier()



def p_attn2(pb):
    nc = pb.nc
    sb = pb.sbuf
    NH, HD, VS = 16, 64, 65
    NTT = S // 128
    xnT = [sb[f"xnT{k}"] for k in range(ND)]
    vbuf = sb["hT"]
    qkw = sb["pt1"]
    ptile = [sb["pt0"][:, i_ * 512:(i_ + 1) * 512] for i_ in range(4)]
    qkT = sb["mvA0"]
    oT = sb["mvA1"]
    zrow_f = sb["ob0"][0:1, 0:512]
    zrow_b = sb["silg"][0:1, 0:512]
    zrep = sb["ob1"][0:64, 0:512]
    trib = sb["silg"][:, 512:640]
    pv = [pb.psum["pf0"], pb.psum["pf1"]]
    pS = [pb.psum["pf2"], pb.psum["pf3"], pb.psum["pf0"], pb.psum["pf1"]]
    pav = pb.psum["pf4"]
    prz = pb.psum["pf5"]

    s_w = pb.sem("a2w")
    s_mm = pb.sem("a2mm")
    s_ev = pb.sem("a2ev")
    s_S = pb.sem("a2S")
    s_x = pb.sem("a2x")
    s_pm = pb.sem("a2pm")
    s_av = pb.sem("a2av")
    s_rz = pb.sem("a2rz")
    s_rp = pb.sem("a2rp")
    s_o = pb.sem("a2o")
    s_st = pb.sem("a2st")

    @pb.sec("gpsimd")
    def _(g):
        s_w.inc(g.dma_start(out=trib, in_=pb.dram["trimask"][:]), 16)

    for s_ in range(BPC):
        base = s_ * S
        wv_t = s_w.n + 16 * ND
        av_before = s_o.n

        @pb.sec("gpsimd")
        def _(g, wv_t=wv_t, av_before=av_before):
            g.wait_ge(s_o.h, av_before)
            for k in range(ND):
                g.dma_start(out=sb["wB"][:, k * 1024:(k + 1) * 1024],
                            in_=pb.dram["wv2t"][k * 128:(k + 1) * 128, :]).then_inc(s_w.h, 16)
            s_w.n += 16 * ND

        vg_t = {}
        for g_i in range(2 * NTT):
            vg_t[g_i] = s_mm.n + g_i + 1
        ev0 = s_ev.n

        @pb.sec("tensor")
        def _(t, base=base, wv_t=wv_t, ev0=ev0, av_before=av_before, xv_t=s_x.n):
            t.wait_ge(s_w.h, wv_t)
            t.wait_ge(s_x.h, xv_t)
            for g_i in range(2 * NTT):
                tt, n = divmod(g_i, 2)
                if g_i >= 2:
                    t.wait_ge(s_ev.h, ev0 + g_i - 1)
                for k in range(ND):
                    ins = nc.tensor.matmul(pv[g_i % 2][:],
                                           xnT[k][:, base + tt * 128:base + (tt + 1) * 128],
                                           sb["wB"][:, k * 1024 + n * 512:k * 1024 + (n + 1) * 512],
                                           start=(k == 0), stop=(k == ND - 1))
                s_mm.inc(ins, 1)

        @pb.sec("vector")
        def _(v, vg_t=vg_t, av_before=av_before):
            v.wait_ge(s_o.h, av_before)
            for tt in range(NTT):
                for h in range(NH):
                    ins = v.memset(vbuf[:, tt * 1040 + h * VS + 64:tt * 1040 + h * VS + 65], 1.0)
            for g_i in range(2 * NTT):
                tt, n = divmod(g_i, 2)
                v.wait_ge(s_mm.h, vg_t[g_i])
                for hh in range(8):
                    h = n * 8 + hh
                    ins = v.tensor_copy(vbuf[:, tt * 1040 + h * VS:tt * 1040 + h * VS + 64],
                                        pv[g_i % 2][:, hh * 64:(hh + 1) * 64])
                s_ev.inc(ins, 1)
        v_done = s_ev.n

        for m in range(8):
            qk_t = s_w.n + 32
            st_prev = s_st.n

            @pb.sec("gpsimd")
            def _(g, m=m, qk_t=qk_t):
                g.wait_ge(s_mm.h, s_mm.n)
                g.dma_start(out=qkw[:, 0:1024],
                            in_=pb.dram["wqkv2t"][m].rearrange("p k c -> p (k c)")).then_inc(s_w.h, 16)
                g.dma_start(out=qkw[:, 1024:2048],
                            in_=pb.dram["wqkv2t"][m + 8].rearrange("p k c -> p (k c)")).then_inc(s_w.h, 16)
                s_w.n += 32

            qk_mm = {}
            for g_i in range(8):
                qk_mm[g_i] = s_mm.n + g_i + 1
            evq0 = s_ev.n
            prev_S_last = s_S.n

            @pb.sec("tensor")
            def _(t, base=base, m=m, qk_t=qk_t, evq0=evq0, v_done=v_done, xq_t=s_x.n):
                t.wait_ge(s_w.h, qk_t)
                t.wait_ge(s_ev.h, v_done)
                t.wait_ge(s_x.h, xq_t)
                for g_i in range(8):
                    io_, cc = divmod(g_i, 4)
                    if g_i >= 2:
                        t.wait_ge(s_ev.h, evq0 + g_i - 1)
                    for k in range(ND):
                        ins = nc.tensor.matmul(pv[g_i % 2][:],
                                               qkw[:, io_ * 1024 + k * 128:io_ * 1024 + (k + 1) * 128],
                                               xnT[k][:, base + cc * 512:base + (cc + 1) * 512],
                                               start=(k == 0), stop=(k == ND - 1))
                    s_mm.inc(ins, 1)

            @pb.sec("vector")
            def _(v, qk_mm=qk_mm, prev_S_last=prev_S_last):
                v.wait_ge(s_S.h, prev_S_last)
                for g_i in range(8):
                    io_, cc = divmod(g_i, 4)
                    v.wait_ge(s_mm.h, qk_mm[g_i])
                    s_ev.inc(v.tensor_copy(qkT[:, io_ * 2048 + cc * 512:io_ * 2048 + (cc + 1) * 512],
                                           pv[g_i % 2][:]), 1)
            qk_done = s_ev.n

            S_t, x_t, pm_t, av_t, rz_t, rp_t, o_t = {}, {}, {}, {}, {}, {}, {}
            cS, cx, cpm, cav, crz, crp, co = (s_S.n, s_x.n, s_pm.n, s_av.n,
                                              s_rz.n, s_rp.n, s_o.n)
            for hh in range(2):
                for ic in range(4):
                    for jb in range(4 * ic + 4):
                        cS += 1
                        S_t[(hh, ic, jb)] = cS
                        cx += 1
                        x_t[(hh, ic, jb)] = cx
                        if jb >= 4 * ic:
                            cpm += 1
                            pm_t[(hh, ic, jb)] = cpm
                        cav += 1
                        av_t[(hh, ic, jb)] = cav
                    crz += 1
                    rz_t[(hh, ic)] = crz
                    crp += 1
                    rp_t[(hh, ic)] = crp
                    co += 1
                    o_t[(hh, ic)] = co

            @pb.sec("tensor")
            def _(t, base=base, m=m, qk_done=qk_done, S_t=S_t, x_t=x_t, pm_t=pm_t,
                  av_t=av_t, rz_t=rz_t, rp_t=rp_t, o_t=o_t):
                LAG = 3
                t.wait_ge(s_ev.h, qk_done)
                for hh in range(2):
                    hrow = hh * 64
                    for ic in range(4):
                        njb = 4 * ic + 4
                        if (hh, ic) != (0, 0):
                            prev = (hh, ic - 1) if ic >= 1 else (hh - 1, 3)
                            t.wait_ge(s_o.h, o_t[prev])
                        for step in range(njb + LAG):
                            if step < njb:
                                jb = step
                                pj = jb % 4
                                if jb >= 4:
                                    t.wait_ge(s_x.h, x_t[(hh, ic, jb - 4)])
                                ins = nc.tensor.matmul(
                                    pS[pj][:], qkT[hrow:hrow + 64, 2048 + jb * 128:2048 + (jb + 1) * 128],
                                    qkT[hrow:hrow + 64, ic * 512:(ic + 1) * 512], start=True, stop=True)
                                s_S.inc(ins, 1)
                            if step >= LAG:
                                jb2 = step - LAG
                                pj2 = jb2 % 4
                                if jb2 >= 4 * ic:
                                    t.wait_ge(s_pm.h, pm_t[(hh, ic, jb2)])
                                else:
                                    t.wait_ge(s_x.h, x_t[(hh, ic, jb2)])
                                h = 2 * m + hh
                                ins = nc.tensor.matmul(pav[0:65, :],
                                                       vbuf[:, jb2 * 1040 + h * VS:jb2 * 1040 + (h + 1) * VS],
                                                       ptile[pj2], start=(jb2 == 0), stop=(jb2 == njb - 1))
                                s_av.inc(ins, 1)
                        t.wait_ge(s_rz.h, rz_t[(hh, ic)])
                        s_rp.inc(nc.tensor.matmul(prz[0:64, :], sb["onesb"][0:1, 0:64],
                                                  zrow_b, start=True, stop=True), 1)

            @pb.sec("scalar")
            def _(sc, S_t=S_t, x_t=x_t, pm_t=pm_t, av_t=av_t):
                for hh in range(2):
                    for ic in range(4):
                        njb = 4 * ic + 4
                        for jb in range(njb):
                            pj = jb % 4
                            sc.wait_ge(s_S.h, S_t[(hh, ic, jb)])
                            occ = (hh, ic, jb - 4)
                            if occ[2] >= 0:
                                sc.wait_ge(s_av.h, av_t[occ])
                            elif (hh, ic) != (0, 0):
                                prev = (hh, ic - 1) if ic >= 1 else (hh - 1, 3)
                                sc.wait_ge(s_av.h, av_t[(prev[0], prev[1], 4 * prev[1] + 4 + jb - 4)])
                            s_x.inc(sc.activation(out=ptile[pj], in_=pS[pj][:], func=AF.Exp,
                                                  scale=0.125), 1)

            @pb.sec("vector")
            def _(v, x_t=x_t, av_t=av_t, rz_t=rz_t, rp_t=rp_t, o_t=o_t, pm_t=pm_t,
                  st_prev=st_prev, m=m):
                for hh in range(2):
                    hrow = hh * 64
                    for ic in range(4):
                        njb = 4 * ic + 4
                        for jb in range(njb):
                            if jb < 4 * ic:
                                continue
                            pj = jb % 4
                            v.wait_ge(s_x.h, x_t[(hh, ic, jb)])
                            inv = min(jb, 4 * ic + 4) - 4 * ic
                            ins = None
                            if inv > 0:
                                ins = v.memset(ptile[pj][:, 0:inv * 128], 0.0)
                            if 4 * ic <= jb <= 4 * ic + 3:
                                dcol = (jb - 4 * ic) * 128
                                ins = v.tensor_tensor(out=ptile[pj][:, dcol:dcol + 128],
                                                      in0=ptile[pj][:, dcol:dcol + 128],
                                                      in1=trib, op=ALU.mult)
                            s_pm.inc(ins, 1)
                        v.wait_ge(s_av.h, av_t[(hh, ic, njb - 1)])
                        pb.vsync(v, v.tensor_copy(zrow_f, pav[64:65, :]))
                        pb.vsync(v, v.reciprocal(zrow_f, zrow_f))
                        s_rz.inc(v.tensor_copy(zrow_b, zrow_f), 1)
                        v.wait_ge(s_rp.h, rp_t[(hh, ic)])
                        pb.vsync(v, v.tensor_copy(zrep, prz[0:64, :]))
                        if st_prev > 0 and (hh, ic) == (0, 0):
                            v.wait_ge(s_st.h, st_prev)
                        s_o.inc(v.tensor_tensor(out=oT[hrow:hrow + 64, ic * 512:(ic + 1) * 512],
                                                in0=pav[0:64, :], in1=zrep, op=ALU.mult), 1)

            @pb.sec("sync")
            def _(s, base=base, m=m, o_t=o_t):
                s.wait_ge(s_o.h, o_t[(1, 3)])
                s_st.inc(s.dma_start(out=pb.dram["acc_aT"][m * 128:(m + 1) * 128, base:base + S],
                                     in_=oT[:, 0:S]), 16)

    a_done = s_st.n
    wo_t = s_w.n + 16 * ND

    @pb.sec("gpsimd")
    def _(g, wo_t=wo_t):
        g.wait_ge(s_mm.h, s_mm.n)
        for k in range(ND):
            g.dma_start(out=sb["wB"][:, k * 1024:(k + 1) * 1024],
                        in_=pb.dram["wo2t"][k * 128:(k + 1) * 128, :]).then_inc(s_w.h, 16)
        s_w.n += 16 * ND

    s_hl = pb.sem("a2hl")

    for s_ in range(BPC):
        base = s_ * S
        hl_t = s_hl.n + 16 * ND

        @pb.sec("gpsimd")
        def _(g, base=base, hl_t=hl_t, a_done=a_done, s_=s_):
            g.wait_ge(s_st.h, a_done)
            if s_ > 0:
                g.wait_ge(s_mm.h, s_mm.n)
            for k in range(ND):
                g.dma_start(out=sb["hT"][:, k * 2048:(k + 1) * 2048],
                            in_=pb.dram["acc_aT"][k * 128:(k + 1) * 128, base:base + S]).then_inc(s_hl.h, 16)
            s_hl.n += 16 * ND

        ob_st = [sb["ob0"][:], sb["ob1"][:]]
        s_stB = [pb.sem("a2sb_0"), pb.sem("a2sb_1")]
        og_t, oe_t, ost_t = {}, {}, {}
        cmm2, cev2 = s_mm.n, s_ev.n
        cst2 = [s_stB[0].n, s_stB[1].n]
        for tt in range(NTT):
            for n in range(2):
                cmm2 += 1
                og_t[(tt, n)] = cmm2
                cev2 += 1
                oe_t[(tt, n)] = cev2
            cst2[tt % 2] += 16
            ost_t[tt] = cst2[tt % 2]

        prev_oe_last = s_ev.n

        @pb.sec("tensor")
        def _(t, hl_t=hl_t, og_t=og_t, oe_t=oe_t, wo_t=wo_t, prev_oe_last=prev_oe_last,
              xB_t=s_x.n):
            t.wait_ge(s_hl.h, hl_t)
            t.wait_ge(s_w.h, wo_t)
            t.wait_ge(s_x.h, xB_t)
            t.wait_ge(s_ev.h, prev_oe_last)
            for tt in range(NTT):
                for n in range(2):
                    g_i = tt * 2 + n
                    if g_i >= 2:
                        t.wait_ge(s_ev.h, oe_t[((g_i - 2) // 2, (g_i - 2) % 2)])
                    for k in range(ND):
                        ins = nc.tensor.matmul(pv[g_i % 2][:],
                                               sb["hT"][:, k * 2048 + tt * 128:k * 2048 + (tt + 1) * 128],
                                               sb["wB"][:, k * 1024 + n * 512:k * 1024 + (n + 1) * 512],
                                               start=(k == 0), stop=(k == ND - 1))
                    s_mm.inc(ins, 1)

        prev_stB = [s_stB[0].n, s_stB[1].n]

        @pb.sec("vector")
        def _(v, og_t=og_t, ost_t=ost_t, s_=s_, prev_stB=prev_stB):
            for tt in range(NTT):
                for n in range(2):
                    v.wait_ge(s_mm.h, og_t[(tt, n)])
                    if n == 0:
                        if tt >= 2:
                            v.wait_ge(s_stB[tt % 2].h, ost_t[tt - 2])
                        elif s_ > 0:
                            v.wait_ge(s_stB[tt % 2].h, prev_stB[tt % 2])
                    s_ev.inc(v.tensor_copy(ob_st[tt % 2][:, n * 512:(n + 1) * 512],
                                           pv[(tt * 2 + n) % 2][:]), 1)

        @pb.sec("sync")
        def _(s, base=base, oe_t=oe_t, s_stB=s_stB):
            for tt in range(NTT):
                s.wait_ge(s_ev.h, oe_t[(tt, 1)])
                row = base + tt * 128
                s_stB[tt % 2].inc(s.dma_start(out=pb.dram["acc_a"][row:row + 128, :],
                                              in_=ob_st[tt % 2][:]), 16)

    @pb.sec("sync")
    def _(s):
        s.wait_ge(s_st.h, s_st.n)
        for ss in ("a2sb_0", "a2sb_1"):
            s.wait_ge(pb.sems[ss].h, pb.sems[ss].n)

    pb.barrier()



def p_ffn2(pb):
    nc = pb.nc
    sb = pb.sbuf
    CH = 512
    NCH = T // CH
    M2 = 32
    wslot = [sb[f"xnT{k}"] for k in range(ND)]
    mv = [sb["mvA0"], sb["mvA1"]]
    w2res = sb["rvecs"][:].bitcast(BF16)
    w2res2 = sb["wB"]
    w2sl = [sb["pt0"][:, 0:512], sb["pt0"][:, 512:1024]]

    def w2tile(k2, n):
        if k2 < 16:
            return w2res[:, k2 * 1024 + n * 512:k2 * 1024 + (n + 1) * 512]
        if k2 < 24:
            return w2res2[:, (k2 - 16) * 1024 + n * 512:(k2 - 16) * 1024 + (n + 1) * 512]
        return w2sl[k2 % 2]
    b1t = sb["gen1"][:, 1024:1056]
    b2row = sb["silg"][0:1, 0:1024]
    lgt = sb["scores"]
    wexp = sb["gen0"][:, 0:NT * 8]
    ob_st = [sb["ob0"][:], sb["ob1"][:], sb["gen0"][:, 1024:2048], sb["gen1"][:, 0:1024]]
    st0, st1 = sb["st0"], sb["st1"]
    pg = [pb.psum["pf0"], pb.psum["pf1"]]
    po = [pb.psum[f"pf{i}"] for i in range(2, 6)]

    s_wv = pb.sem("f2w")
    s_w2 = [pb.sem("f2w2_0"), pb.sem("f2w2_1")]
    s_xl = [pb.sem("f2x_0"), pb.sem("f2x_1")]
    s_m1 = pb.sem("f2m1")
    s_g = pb.sem("f2g")
    s_m2 = pb.sem("f2m2")
    s_e = pb.sem("f2e")
    s_al = [pb.sem(f"f2al_{i_}") for i_ in range(4)]
    s_as = [pb.sem(f"f2as_{i_}") for i_ in range(4)]
    s_vd = pb.sem("f2vd")
    s_ad = pb.sem("f2ad")
    s_rt = pb.sem("f2rt")
    vd0, ad0 = s_vd.n, s_ad.n

    @pb.sec("scalar")
    def _(sc):
        for i in range(NT):
            L = lgt[:, i * 8:(i + 1) * 8]
            sc.wait_ge(s_vd.h, vd0 + i + 1)
            sc.activation(out=st0[:, 0:8], in_=L, func=AF.Exp, bias=st1[:, 1:2])
            s_ad.inc(sc.activation(out=st0[:, 8:9], in_=st1[:, 2:3], func=AF.Exp,
                                   bias=st1[:, 1:2]), 1)

    @pb.sec("vector")
    def _(v):
        for i in range(NT):
            L = lgt[:, i * 8:(i + 1) * 8]
            W = wexp[:, i * 8:(i + 1) * 8]
            pb.vsync(v, v.max(out=st1[:, 8:16], in_=L))
            pb.vsync(v, v.tensor_scalar(out=st1[:, 1:2], in0=st1[:, 8:9],
                                        scalar1=-1.0, scalar2=None, op0=ALU.mult))
            s_vd.inc(v.tensor_copy(st1[:, 2:3], st1[:, 9:10]), 1)
            v.wait_ge(s_ad.h, ad0 + i + 1)
            pb.vsync(v, v.reduce_sum(st1[:, 3:4], st0[:, 0:8], axis=AX.X))
            pb.vsync(v, v.reciprocal(st1[:, 4:5], st1[:, 3:4]))
            pb.vsync(v, v.tensor_scalar(out=W, in0=L, scalar1=st1[:, 8:9],
                                        scalar2=None, op0=ALU.is_ge))
            pb.vsync(v, v.tensor_tensor_scan(out=sb["gen1"][:, 1056:1064], data0=sb["small"][:, 0:8],
                                             data1=W, initial=0.0, op0=ALU.mult, op1=ALU.add))
            pb.vsync(v, v.tensor_scalar(out=sb["gen1"][:, 1064:1072], in0=sb["gen1"][:, 1056:1064],
                                        scalar1=1.0, scalar2=None, op0=ALU.is_equal))
            pb.vsync(v, v.tensor_tensor(out=W, in0=W, in1=sb["gen1"][:, 1064:1072], op=ALU.mult))
            pb.vsync(v, v.scalar_tensor_tensor(out=sb["gen1"][:, 1056:1064], in0=W, scalar=-1e30,
                                               in1=L, op0=ALU.mult, op1=ALU.add))
            pb.vsync(v, v.tensor_scalar(out=sb["gen1"][:, 1056:1064], in0=sb["gen1"][:, 1056:1064],
                                        scalar1=st1[:, 2:3], scalar2=None, op0=ALU.is_ge))
            pb.vsync(v, v.tensor_tensor_scan(out=sb["gen1"][:, 1064:1072], data0=sb["small"][:, 0:8],
                                             data1=sb["gen1"][:, 1056:1064], initial=0.0,
                                             op0=ALU.mult, op1=ALU.add))
            pb.vsync(v, v.tensor_scalar(out=sb["gen1"][:, 1064:1072], in0=sb["gen1"][:, 1064:1072],
                                        scalar1=1.0, scalar2=None, op0=ALU.is_equal))
            pb.vsync(v, v.tensor_tensor(out=sb["gen1"][:, 1056:1064], in0=sb["gen1"][:, 1056:1064],
                                        in1=sb["gen1"][:, 1064:1072], op=ALU.mult))
            pb.vsync(v, v.tensor_tensor(out=st1[:, 5:6], in0=st1[:, 4:5],
                                        in1=st0[:, 8:9], op=ALU.mult))
            pb.vsync(v, v.tensor_scalar(out=W, in0=W, scalar1=st1[:, 4:5],
                                        scalar2=None, op0=ALU.mult))
            s_rt.inc(v.scalar_tensor_tensor(out=W, in0=sb["gen1"][:, 1056:1064],
                                            scalar=st1[:, 5:6], in1=W,
                                            op0=ALU.mult, op1=ALU.add), 1)

    m1_t, g_t, m2_t, e_t, xl_t, w2_t, al_t, as_t, w1_t = {}, {}, {}, {}, {}, {}, {}, {}, {}
    cm1, cg, cm2, ce = s_m1.n, s_g.n, s_m2.n, s_e.n
    cxl = [s_xl[0].n, s_xl[1].n]
    cw2 = [s_w2[0].n, s_w2[1].n]
    cal = [x.n for x in s_al]
    cas = [x.n for x in s_as]
    cw1 = s_wv.n
    for e in range(8):
        cw1 += 16 * (ND + 2 + 24)
        w1_t[e] = cw1
        for c in range(NCH):
            cxl[c % 2] += 16 * ND
            xl_t[(e, c)] = cxl[c % 2]
            for mm in range(M2):
                cm1 += 1
                m1_t[(e, c, mm)] = cm1
                cg += 1
                g_t[(e, c, mm)] = cg
            for n in range(2):
                for k2 in range(M2):
                    if k2 >= 24:
                        cw2[k2 % 2] += 16
                        w2_t[(e, c, n, k2)] = cw2[k2 % 2]
                    cm2 += 1
                    m2_t[(e, c, n, k2)] = cm2
            for n in range(2):
                for mt in range(4):
                    ce += 1
                    e_t[(e, c, n, mt)] = ce
            for mt in range(4):
                if e > 0:
                    cal[mt] += 16
                    al_t[(e, c, mt)] = cal[mt]
                cas[mt] += 16
                as_t[(e, c, mt)] = cas[mt]

    def prev_mt(e, c, mt):
        if c > 0:
            return (e, c - 1, mt)
        if e > 0:
            return (e - 1, NCH - 1, mt)
        return None

    @pb.sec("gpsimd")
    def _(g):
        for e in range(8):
            if e > 0:
                g.wait_ge(s_m2.h, m2_t[(e - 1, NCH - 1, 1, M2 - 1)])
                g.wait_ge(s_g.h, g_t[(e - 1, NCH - 1, M2 - 1)])
            for k in range(ND):
                g.dma_start(out=wslot[k][:],
                            in_=pb.dram["w21t"][e, k].rearrange("p m c -> p (m c)")
                            ).then_inc(s_wv.h, 16)
            g.dma_start(out=b1t, in_=pb.dram["b21"][e]).then_inc(s_wv.h, 16)
            g.dma_start(out=b2row, in_=pb.dram["b22"][e:e + 1, :]).then_inc(s_wv.h, 16)
            for k2 in range(24):
                dst = (w2res[:, k2 * 1024:(k2 + 1) * 1024] if k2 < 16 else
                       w2res2[:, (k2 - 16) * 1024:(k2 - 15) * 1024])
                g.dma_start(out=dst, in_=pb.dram["w22t"][e, k2 * 128:(k2 + 1) * 128, :]
                            ).then_inc(s_wv.h, 16)
            s_wv.n += 16 * (ND + 2 + 24)
            for c in range(NCH):
                for n in range(2):
                    for k2 in range(24, M2):
                        pk = k2 % 2
                        occ = (e, c, n, k2 - 2)
                        if k2 < 26:
                            if n > 0:
                                occ = (e, c, 0, M2 - 2 + k2 - 24)
                            elif c > 0:
                                occ = (e, c - 1, 1, M2 - 2 + k2 - 24)
                            elif e > 0:
                                occ = (e - 1, NCH - 1, 1, M2 - 2 + k2 - 24)
                            else:
                                occ = None
                        if occ in m2_t:
                            g.wait_ge(s_m2.h, m2_t[occ])
                        g.dma_start(out=w2sl[pk],
                                    in_=pb.dram["w22t"][e, k2 * 128:(k2 + 1) * 128,
                                                        n * 512:(n + 1) * 512]).then_inc(s_w2[pk].h, 16)
                        s_w2[pk].n += 16

    @pb.sec("sync")
    def _(s):
        for e in range(8):
            for c in range(NCH):
                pc = c % 2
                occ = (e, c - 2) if c >= 2 else ((e - 1, NCH - 2 + c) if e > 0 else None)
                if occ is not None:
                    s.wait_ge(s_m1.h, m1_t[occ + (M2 - 1,)])
                for k in range(ND):
                    s.dma_start(out=mv[pc][:, k * 512:(k + 1) * 512],
                                in_=pb.dram["xnT"][k * 128:(k + 1) * 128,
                                                   c * CH:(c + 1) * CH]).then_inc(s_xl[pc].h, 16)
                s_xl[pc].n += 16 * ND
                if e > 0:
                    for mt in range(4):
                        row = (c * 4 + mt) * 128
                        pv_ = prev_mt(e, c, mt)
                        if pv_ in as_t:
                            s.wait_ge(s_as[mt].h, as_t[pv_])
                        s.dma_start(out=ob_st[mt][:],
                                    in_=pb.dram["acc_f"][row:row + 128, :]).then_inc(s_al[mt].h, 16)
                        s_al[mt].n += 16
                for mt in range(4):
                    row = (c * 4 + mt) * 128
                    s.wait_ge(s_e.h, e_t[(e, c, 1, mt)])
                    s.dma_start(out=pb.dram["acc_f"][row:row + 128, :],
                                in_=ob_st[mt][:]).then_inc(s_as[mt].h, 16)
                    s_as[mt].n += 16
        s.wait_ge(s_as[0].h, s_as[0].n)
        s.wait_ge(s_as[1].h, s_as[1].n)

    @pb.sec("tensor")
    def _(t):
        for e in range(8):
            t.wait_ge(s_wv.h, w1_t[e])
            for c in range(NCH):
                pc = c % 2
                t.wait_ge(s_xl[pc].h, xl_t[(e, c)])
                for mm in range(M2):
                    pm_ = mm % 2
                    occ = (e, c, mm - 2)
                    if mm < 2:
                        occ = (e, c - 1, M2 - 2 + mm) if c > 0 else \
                              ((e - 1, NCH - 1, M2 - 2 + mm) if e > 0 else None)
                    if occ in g_t:
                        t.wait_ge(s_g.h, g_t[occ])
                    for k in range(ND):
                        ins = nc.tensor.matmul(pg[pm_][:], wslot[k][:, mm * 128:(mm + 1) * 128],
                                               mv[pc][:, k * 512:(k + 1) * 512],
                                               start=(k == 0), stop=(k == ND - 1))
                    s_m1.inc(ins, 1)
                for n in range(2):
                    for k2 in range(M2):
                        if k2 >= 24:
                            t.wait_ge(s_w2[k2 % 2].h, w2_t[(e, c, n, k2)])
                        if k2 == 0:
                            t.wait_ge(s_g.h, g_t[(e, c, M2 - 1)])
                            occ = (e, c, 0) if n > 0 else \
                                  ((e, c - 1, 1) if c > 0 else
                                   ((e - 1, NCH - 1, 1) if e > 0 else None))
                            if occ is not None:
                                t.wait_ge(s_e.h, e_t[occ + (3,)])
                        for mt in range(4):
                            ins = nc.tensor.matmul(po[mt][:],
                                                   sb["hT"][:, k2 * 512 + mt * 128:k2 * 512 + (mt + 1) * 128],
                                                   w2tile(k2, n), start=(k2 == 0), stop=False)
                        if k2 == M2 - 1:
                            for mt in range(4):
                                ins = nc.tensor.matmul(po[mt][:], sb["onesb"][0:1, 0:128],
                                                       b2row[:, n * 512:(n + 1) * 512],
                                                       start=False, stop=True)
                        s_m2.inc(ins, 1)

    @pb.sec("scalar")
    def _(sc):
        for e in range(8):
            for c in range(NCH):
                for mm in range(M2):
                    pm_ = mm % 2
                    sc.wait_ge(s_m1.h, m1_t[(e, c, mm)])
                    if c > 0 or e > 0:
                        pe_, pc_ = (e, c - 1) if c > 0 else (e - 1, NCH - 1)
                        sc.wait_ge(s_m2.h, m2_t[(pe_, pc_, 1, M2 - 1)])
                    s_g.inc(sc.activation(out=sb["hT"][:, mm * 512:(mm + 1) * 512],
                                          in_=pg[pm_][:], func=AF.Gelu,
                                          bias=b1t[:, mm:mm + 1]), 1)

    @pb.sec("vector")
    def _(v):
        for e in range(8):
            for c in range(NCH):
                for n in range(2):
                    for mt in range(4):
                        pm = mt
                        v.wait_ge(s_m2.h, m2_t[(e, c, n, M2 - 1)])
                        wcol = wexp[:, (c * 4 + mt) * 8 + e:(c * 4 + mt) * 8 + e + 1]
                        if e > 0:
                            if n == 0:
                                v.wait_ge(s_al[pm].h, al_t[(e, c, mt)])
                            ins = v.scalar_tensor_tensor(out=ob_st[pm][:, n * 512:(n + 1) * 512],
                                                         in0=po[mt][:], scalar=wcol,
                                                         in1=ob_st[pm][:, n * 512:(n + 1) * 512],
                                                         op0=ALU.mult, op1=ALU.add)
                        else:
                            if n == 0:
                                pv_ = prev_mt(e, c, mt)
                                if pv_ in as_t:
                                    v.wait_ge(s_as[pm].h, as_t[pv_])
                            ins = v.tensor_scalar(out=ob_st[pm][:, n * 512:(n + 1) * 512],
                                                  in0=po[mt][:], scalar1=wcol, scalar2=None,
                                                  op0=ALU.mult)
                        s_e.inc(ins, 1)

    pb.barrier()



def p_ffn3(pb):
    nc = pb.nc
    sb = pb.sbuf
    CH = 512
    NCH = T // CH
    M3 = 4
    mv = [sb["mvA0"], sb["mvA1"]]
    wgb = sb["wB"][:, 0:4096]
    wub = sb["wB"][:, 4096:8192]
    wdsl = [sb["pt1"][:, 0:1024], sb["pt1"][:, 1024:2048]]
    srrow = sb["silg"][0:1, 0:512]
    srep = sb["ob0"]
    gs = sb["hT"][:, 8192:8704]
    sgs = sb["hT"][:, 8704:9216]
    us = sb["hT"][:, 9216:9728]
    lgt = sb["scores"]
    sexp = sb["gen0"][:, 0:NT * 8]
    ob_st = [sb["ob1"][:], sb["gen0"][:, 1024:2048], sb["gen1"][:, 0:1024],
             sb["gen1"][:, 1024:2048]]
    st0, st1 = sb["st0"], sb["st1"]
    pg = [pb.psum["pf0"], pb.psum["pf1"]]
    po = [pb.psum[f"pf{i}"] for i in range(2, 6)]
    prz = pb.psum["pf1"]

    s_wv = pb.sem("f3w")
    s_wd = [pb.sem("f3wd_0"), pb.sem("f3wd_1")]
    s_xl = [pb.sem("f3x_0"), pb.sem("f3x_1")]
    s_sr = pb.sem("f3sr")
    s_rp = pb.sem("f3rp")
    s_rc = pb.sem("f3rc")
    s_m1 = pb.sem("f3m1")
    s_gu = pb.sem("f3gu")
    s_sl = pb.sem("f3sl")
    s_m2 = pb.sem("f3m2")
    s_e = pb.sem("f3e")
    s_al = [pb.sem(f"f3al_{i_}") for i_ in range(4)]
    s_as = [pb.sem(f"f3as_{i_}") for i_ in range(4)]
    s_rt = pb.sem("f3rt")
    s_rd = pb.sem("f3rd")

    s_vd = pb.sem("f3vd")
    s_ad = pb.sem("f3ad")
    vd0, ad0 = s_vd.n, s_ad.n

    @pb.sec("scalar")
    def _(sc):
        for i in range(NT):
            sc.wait_ge(s_vd.h, vd0 + i + 1)
            s_ad.inc(sc.activation(out=st1[:, 2:3], in_=st1[:, 0:1], func=AF.Sigmoid), 1)

    @pb.sec("vector")
    def _(v):
        for i in range(NT):
            L = lgt[:, i * 8:(i + 1) * 8]
            W = sexp[:, i * 8:(i + 1) * 8]
            s_vd.inc(v.reduce_max(st1[:, 0:1], L, axis=AX.X), 1)
            v.wait_ge(s_vd.h, vd0 + i + 1)
            pb.vsync(v, v.tensor_scalar(out=W, in0=L, scalar1=st1[:, 0:1],
                                        scalar2=None, op0=ALU.is_ge))
            pb.vsync(v, v.tensor_tensor_scan(out=st0[:, 0:8], data0=sb["small"][:, 0:8],
                                             data1=W, initial=0.0, op0=ALU.mult, op1=ALU.add))
            pb.vsync(v, v.tensor_scalar(out=st0[:, 8:16], in0=st0[:, 0:8],
                                        scalar1=1.0, scalar2=None, op0=ALU.is_equal))
            pb.vsync(v, v.tensor_tensor(out=W, in0=W, in1=st0[:, 8:16], op=ALU.mult))
            v.wait_ge(s_ad.h, ad0 + i + 1)
            s_rt.inc(v.tensor_scalar(out=W, in0=W, scalar1=st1[:, 2:3],
                                     scalar2=None, op0=ALU.mult), 1)

    @pb.sec("sync")
    def _(s):
        s.wait_ge(s_rt.h, s_rt.n)
        for r in range(8):
            src = sexp.rearrange("p (i r) -> p r i", r=8)[:, r]
            dst = pb.dram["rmask"][r:r + 1, :].rearrange("o (i p) -> (o p) i", p=128)
            s_rd.inc(s.dma_start(out=dst, in_=src, allow_slow_non_contiguous=True), 16)

    m1_t, gu_t, sl_t, m2_t, e_t, xl_t, wd_t, al_t, as_t, w1_t = {}, {}, {}, {}, {}, {}, {}, {}, {}, {}
    sr_t, rp_t, rc_t = {}, {}, {}
    cm1, cgu, csl, cm2, ce = s_m1.n, s_gu.n, s_sl.n, s_m2.n, s_e.n
    csr, crp, crc = s_sr.n, s_rp.n, s_rc.n
    cxl = [s_xl[0].n, s_xl[1].n]
    cwd = [s_wd[0].n, s_wd[1].n]
    cal = [x.n for x in s_al]
    cas = [x.n for x in s_as]
    cw1 = s_wv.n
    for e in range(8):
        cw1 += 16 * 16
        w1_t[e] = cw1
        for c in range(NCH):
            cxl[c % 2] += 16 * ND
            xl_t[(e, c)] = cxl[c % 2]
            csr += 16
            sr_t[(e, c)] = csr
            crp += 1
            rp_t[(e, c)] = crp
            crc += 1
            rc_t[(e, c)] = crc
            for mm in range(M3):
                cm1 += 2
                m1_t[(e, c, mm)] = cm1
                cgu += 1
                gu_t[(e, c, mm, "gs")] = cgu
                csl += 1
                sl_t[(e, c, mm)] = csl
                cgu += 1
                gu_t[(e, c, mm, "us")] = cgu
                cgu += 1
                gu_t[(e, c, mm, "h")] = cgu
            for n in range(2):
                for mm in range(M3):
                    cwd[mm % 2] += 16
                    wd_t[(e, c, n, mm)] = cwd[mm % 2]
                    cm2 += 1
                    m2_t[(e, c, n, mm)] = cm2
            for n in range(2):
                for mt in range(4):
                    ce += 1
                    e_t[(e, c, n, mt)] = ce
            for mt in range(4):
                if e > 0:
                    cal[mt] += 16
                    al_t[(e, c, mt)] = cal[mt]
                cas[mt] += 16
                as_t[(e, c, mt)] = cas[mt]

    def prev_mt(e, c, mt):
        if c > 0:
            return (e, c - 1, mt)
        if e > 0:
            return (e - 1, NCH - 1, mt)
        return None

    @pb.sec("gpsimd")
    def _(g):
        for e in range(8):
            if e > 0:
                g.wait_ge(s_m1.h, m1_t[(e - 1, NCH - 1, M3 - 1)])
            for k in range(ND):
                g.dma_start(out=wgb[:, k * 512:(k + 1) * 512],
                            in_=pb.dram["wg3f"][e, k].rearrange("p m c -> p (m c)")
                            ).then_inc(s_wv.h, 16)
                g.dma_start(out=wub[:, k * 512:(k + 1) * 512],
                            in_=pb.dram["wu3f"][e, k].rearrange("p m c -> p (m c)")
                            ).then_inc(s_wv.h, 16)
            s_wv.n += 16 * 16
            for c in range(NCH):
                g.wait_ge(s_rd.h, s_rd.n)
                occ = (e, c - 1) if c > 0 else ((e - 1, NCH - 1) if e > 0 else None)
                if occ in rp_t:
                    g.wait_ge(s_rp.h, rp_t[occ])
                g.dma_start(out=srrow, in_=pb.dram["rmask"][e:e + 1, c * 512:(c + 1) * 512]
                            ).then_inc(s_sr.h, 16)
                s_sr.n += 16
                for n in range(2):
                    for mm in range(M3):
                        pk = mm % 2
                        occ2 = (e, c, n, mm - 2)
                        if mm < 2:
                            if n > 0:
                                occ2 = (e, c, 0, 2 + mm)
                            elif c > 0:
                                occ2 = (e, c - 1, 1, 2 + mm)
                            elif e > 0:
                                occ2 = (e - 1, NCH - 1, 1, 2 + mm)
                            else:
                                occ2 = None
                        if occ2 in m2_t:
                            g.wait_ge(s_m2.h, m2_t[occ2])
                        g.dma_start(out=wdsl[pk],
                                    in_=pb.dram["wd3f"][e, mm * 128:(mm + 1) * 128, :]
                                    ).then_inc(s_wd[pk].h, 16)
                        s_wd[pk].n += 16

    @pb.sec("sync")
    def _(s):
        for e in range(8):
            for c in range(NCH):
                pc = c % 2
                occ = (e, c - 2) if c >= 2 else ((e - 1, NCH - 2 + c) if e > 0 else None)
                if occ is not None and occ + (M3 - 1,) in m1_t:
                    s.wait_ge(s_m1.h, m1_t[occ + (M3 - 1,)])
                for k in range(ND):
                    s.dma_start(out=mv[pc][:, k * 512:(k + 1) * 512],
                                in_=pb.dram["xnT"][k * 128:(k + 1) * 128,
                                                   c * CH:(c + 1) * CH]).then_inc(s_xl[pc].h, 16)
                s_xl[pc].n += 16 * ND
                if e > 0:
                    for mt in range(4):
                        row = (c * 4 + mt) * 128
                        pv_ = prev_mt(e, c, mt)
                        if pv_ in as_t:
                            s.wait_ge(s_as[mt].h, as_t[pv_])
                        s.dma_start(out=ob_st[mt][:],
                                    in_=pb.dram["acc_f"][row:row + 128, :]).then_inc(s_al[mt].h, 16)
                        s_al[mt].n += 16
                for mt in range(4):
                    row = (c * 4 + mt) * 128
                    s.wait_ge(s_e.h, e_t[(e, c, 1, mt)])
                    s.dma_start(out=pb.dram["acc_f"][row:row + 128, :],
                                in_=ob_st[mt][:]).then_inc(s_as[mt].h, 16)
                    s_as[mt].n += 16
        for x in s_as:
            s.wait_ge(x.h, x.n)

    @pb.sec("tensor")
    def _(t):
        for e in range(8):
            t.wait_ge(s_wv.h, w1_t[e])
            for c in range(NCH):
                pc = c % 2
                t.wait_ge(s_xl[pc].h, xl_t[(e, c)])
                t.wait_ge(s_sr.h, sr_t[(e, c)])
                occ = (e, c - 1) if c > 0 else ((e - 1, NCH - 1) if e > 0 else None)
                if occ in rc_t:
                    t.wait_ge(s_rc.h, rc_t[occ])
                if occ is not None:
                    t.wait_ge(s_gu.h, gu_t[occ + (M3 - 1, "us")])
                s_rp.inc(nc.tensor.matmul(prz[:], sb["onesb"][0:1, :],
                                          srrow, start=True, stop=True), 1)
                t.wait_ge(s_rc.h, rc_t[(e, c)])
                for mm in range(M3):
                    if mm > 0:
                        t.wait_ge(s_gu.h, gu_t[(e, c, mm - 1, "gs")])
                        t.wait_ge(s_gu.h, gu_t[(e, c, mm - 1, "us")])
                    elif occ is not None:
                        t.wait_ge(s_gu.h, gu_t[occ + (M3 - 1, "gs")])
                        t.wait_ge(s_gu.h, gu_t[occ + (M3 - 1, "us")])
                    for k in range(ND):
                        ins = nc.tensor.matmul(pg[0][:], wgb[:, k * 512 + mm * 128:k * 512 + (mm + 1) * 128],
                                               mv[pc][:, k * 512:(k + 1) * 512],
                                               start=(k == 0), stop=(k == ND - 1))
                    s_m1.inc(ins, 1)
                    for k in range(ND):
                        ins = nc.tensor.matmul(pg[1][:], wub[:, k * 512 + mm * 128:k * 512 + (mm + 1) * 128],
                                               mv[pc][:, k * 512:(k + 1) * 512],
                                               start=(k == 0), stop=(k == ND - 1))
                    s_m1.inc(ins, 1)
                for n in range(2):
                    for mm in range(M3):
                        pk = mm % 2
                        t.wait_ge(s_wd[pk].h, wd_t[(e, c, n, mm)])
                        if mm == 0:
                            t.wait_ge(s_gu.h, gu_t[(e, c, M3 - 1, "h")])
                            occp = (e, c, 0) if n > 0 else \
                                   ((e, c - 1, 1) if c > 0 else
                                    ((e - 1, NCH - 1, 1) if e > 0 else None))
                            if occp is not None:
                                t.wait_ge(s_e.h, e_t[occp + (3,)])
                        for mt in range(4):
                            ins = nc.tensor.matmul(po[mt][:],
                                                   sb["hT"][:, mm * 512 + mt * 128:mm * 512 + (mt + 1) * 128],
                                                   wdsl[pk][:, n * 512:(n + 1) * 512],
                                                   start=(mm == 0), stop=(mm == M3 - 1))
                        s_m2.inc(ins, 1)

    @pb.sec("scalar")
    def _(sc):
        for e in range(8):
            for c in range(NCH):
                for mm in range(M3):
                    sc.wait_ge(s_gu.h, gu_t[(e, c, mm, "gs")])
                    s_sl.inc(sc.activation(out=sgs, in_=gs, func=AF.Silu), 1)

    @pb.sec("vector")
    def _(v):
        for e in range(8):
            for c in range(NCH):
                v.wait_ge(s_rp.h, rp_t[(e, c)])
                s_rc.inc(v.tensor_copy(srep[:, 0:512], prz[:]), 1)
                for mm in range(M3):
                    v.wait_ge(s_m1.h, m1_t[(e, c, mm)])
                    if mm > 0:
                        v.wait_ge(s_gu.h, gu_t[(e, c, mm - 1, "h")])
                    s_gu.inc(v.tensor_tensor(out=gs, in0=pg[0][:], in1=srep[:, 0:512],
                                             op=ALU.mult), 1)
                    hblk = sb["hT"][:, mm * 512:mm * 512 + 512]
                    occh = (e, c - 1) if c > 0 else ((e - 1, NCH - 1) if e > 0 else None)
                    if mm == 0 and occh is not None:
                        v.wait_ge(s_m2.h, m2_t[occh + (1, M3 - 1)])
                    n_us = s_gu.inc(v.tensor_tensor(out=us, in0=pg[1][:], in1=srep[:, 0:512],
                                                    op=ALU.mult), 1)
                    v.wait_ge(s_gu.h, n_us)
                    v.wait_ge(s_sl.h, sl_t[(e, c, mm)])
                    s_gu.inc(v.tensor_tensor(out=hblk, in0=sgs, in1=us, op=ALU.mult), 1)
                for n in range(2):
                    for mt in range(4):
                        v.wait_ge(s_m2.h, m2_t[(e, c, n, M3 - 1)])
                        if e > 0:
                            if n == 0:
                                v.wait_ge(s_al[mt].h, al_t[(e, c, mt)])
                            ins = v.tensor_tensor(out=ob_st[mt][:, n * 512:(n + 1) * 512],
                                                  in0=po[mt][:],
                                                  in1=ob_st[mt][:, n * 512:(n + 1) * 512],
                                                  op=ALU.add)
                        else:
                            if n == 0:
                                pv_ = prev_mt(e, c, mt)
                                if pv_ in as_t:
                                    v.wait_ge(s_as[mt].h, as_t[pv_])
                            ins = v.tensor_copy(ob_st[mt][:, n * 512:(n + 1) * 512], po[mt][:])
                        s_e.inc(ins, 1)

    pb.barrier()



def p_attn1(pb):
    nc = pb.nc
    sb = pb.sbuf
    NH, HD, VS = 16, 64, 65
    NTT = S // 128
    NJB = 16
    xnT = [sb[f"xnT{k}"] for k in range(ND)]
    vbuf = sb["hT"]
    qkw = sb["pt1"]
    ptile = [sb["pt0"][:, i_ * 512:(i_ + 1) * 512] for i_ in range(4)]
    t1sl = [sb["mvA1"][:, 2048 + i_ * 512:2048 + (i_ + 1) * 512] for i_ in range(4)]
    qkT = sb["mvA0"]
    oT = sb["mvA1"]
    growb = sb["wB"][0:1, 0:2048]
    zrow_f = sb["ob0"][0:1, 0:512]
    zrow_b = sb["silg"][0:1, 512:1024]
    zrep = sb["ob0"][64:128, 512:1024]
    grep = sb["silg"][:, 0:512]
    Ssb = [sb["ob1"][:, 0:512], sb["ob1"][:, 512:1024]]
    bbuf = sb["rvecs"][:].bitcast(BF16)[:, 0:3969]
    gatesT = sb["gen1"][0:2, 0:2048]
    grow_f = sb["gen0"][0:1, 0:2048]
    gsc1 = sb["gen0"][0:1, 256:264]
    b1a = sb["scores"][:, 32:48]
    g1Wt = sb["scores"][0:64, 0:8]
    g1bc = sb["scores"][0:1, 64:88]
    auxf = sb["scores"][:, 90:92].bitcast(BF16)
    pv = [pb.psum["pf0"], pb.psum["pf1"]]
    pS = [pb.psum["pf2"], pb.psum["pf3"], pb.psum["pf0"], pb.psum["pf1"]]
    pav = pb.psum["pf4"]
    prz = pb.psum["pf5"]

    s_w = pb.sem("a1w")
    s_bb = pb.sem("a1bb")
    s_mm = pb.sem("a1mm")
    s_ev = pb.sem("a1ev")
    s_S = pb.sem("a1S")
    s_sb = pb.sem("a1sb")
    s_x = pb.sem("a1x")
    s_av = pb.sem("a1av")
    s_rz = pb.sem("a1rz")
    s_rp = pb.sem("a1rp")
    s_gr = pb.sem("a1gr")
    s_gc = pb.sem("a1gc")
    s_gt = pb.sem("a1gt")
    s_gp = pb.sem("a1gp")
    s_ga = pb.sem("a1ga")
    s_o = pb.sem("a1o")
    s_st = pb.sem("a1st")
    s_c = pb.sem("a1c")

    @pb.sec("gpsimd")
    def _(g):
        s_c.inc(g.dma_start(out=b1a, in_=pb.dram["b1a"][:]), 16)
        s_c.inc(g.dma_start(out=g1Wt, in_=pb.dram["g1W"][:]), 16)
        s_c.inc(g.dma_start(out=g1bc, in_=pb.dram["g1bc"][:]), 16)
    cN = s_c.n

    @pb.sec("vector")
    def _(v):
        v.wait_ge(s_c.h, cN)
        pb.vsync(v, v.reduce_sum(sb["scores"][0:64, 8:9], g1Wt[:, 0:4], axis=AX.X))
        pb.vsync(v, v.reduce_sum(sb["scores"][0:64, 9:10], g1Wt[:, 4:8], axis=AX.X))
        pb.vsync(v, v.reduce_sum(sb["scores"][0:1, 88:89], g1bc[:, 0:4], axis=AX.X))
        s_gt.inc(v.reduce_sum(sb["scores"][0:1, 89:90], g1bc[:, 4:8], axis=AX.X), 1)

    @pb.sec("gpsimd")
    def _(g):
        g.wait_ge(s_gt.h, s_gt.n)
        s_c.inc(g.dma_start(out=pb.dram["gscr"][0:64, 0:2], in_=sb["scores"][0:64, 8:10]), 16)
        s_c.inc(g.dma_start(out=pb.dram["gscr"][64:65, 0:2], in_=sb["scores"][0:1, 88:90]), 16)
        g.wait_ge(s_c.h, s_c.n)
        s_c.inc(g.dma_start(out=auxf[0:64, 0:2], in_=pb.dram["gscr"][0:64, 0:2]), 16)
        s_c.inc(g.dma_start(out=auxf[64:128, 0:2], in_=pb.dram["gscr"][0:64, 0:2]), 16)
        s_c.inc(g.dma_start(out=sb["st1"][0:2, 0:1],
                            in_=pb.dram["gscr"][64:65, 0:2].rearrange("a b -> b a")), 16)
    gt_base = s_c.n

    for s_ in range(BPC):
        base = s_ * S
        wv_t = s_w.n + 16 * ND
        av_before = s_o.n

        @pb.sec("gpsimd")
        def _(g, wv_t=wv_t, av_before=av_before):
            g.wait_ge(s_o.h, av_before)
            for k in range(ND):
                g.dma_start(out=sb["wB"][:, k * 1024:(k + 1) * 1024],
                            in_=pb.dram["wv1t"][k * 128:(k + 1) * 128, :]).then_inc(s_w.h, 16)
            s_w.n += 16 * ND
            g.dma_start(out=qkw[0:1, 0:1024], in_=pb.dram["bv1row"][:]).then_inc(s_w.h, 16)
            s_w.n += 16

        vg_t = {}
        for g_i in range(2 * NTT):
            vg_t[g_i] = s_mm.n + g_i + 1
        ev0 = s_ev.n
        bv_t = s_w.n

        @pb.sec("tensor")
        def _(t, base=base, bv_t=bv_t, ev0=ev0, xv_t=s_x.n):
            t.wait_ge(s_w.h, bv_t)
            t.wait_ge(s_x.h, xv_t)
            for g_i in range(2 * NTT):
                tt, n = divmod(g_i, 2)
                if g_i >= 2:
                    t.wait_ge(s_ev.h, ev0 + g_i - 1)
                for k in range(ND):
                    nc.tensor.matmul(pv[g_i % 2][:],
                                     xnT[k][:, base + tt * 128:base + (tt + 1) * 128],
                                     sb["wB"][:, k * 1024 + n * 512:k * 1024 + (n + 1) * 512],
                                     start=(k == 0), stop=False)
                ins = nc.tensor.matmul(pv[g_i % 2][:], sb["onesb"][0:1, 0:128],
                                       qkw[0:1, n * 512:(n + 1) * 512].bitcast(F32)
                                       if False else qkw[0:1, n * 512:(n + 1) * 512],
                                       start=False, stop=True)
                s_mm.inc(ins, 1)

        @pb.sec("vector")
        def _(v, vg_t=vg_t, av_before=av_before):
            v.wait_ge(s_o.h, av_before)
            for tt in range(NTT):
                for h in range(NH):
                    v.memset(vbuf[:, tt * 1040 + h * VS + 64:tt * 1040 + h * VS + 65], 1.0)
            for g_i in range(2 * NTT):
                tt, n = divmod(g_i, 2)
                v.wait_ge(s_mm.h, vg_t[g_i])
                for hh in range(8):
                    h = n * 8 + hh
                    ins = v.tensor_copy(vbuf[:, tt * 1040 + h * VS:tt * 1040 + h * VS + 64],
                                        pv[g_i % 2][:, hh * 64:(hh + 1) * 64])
                s_ev.inc(ins, 1)
        v_done = s_ev.n

        for m in range(8):
            qk_t = s_w.n + 32
            st_prev = s_st.n

            @pb.sec("gpsimd")
            def _(g, m=m):
                g.wait_ge(s_mm.h, s_mm.n)
                g.dma_start(out=qkw[:, 0:1024],
                            in_=pb.dram["wq1at"][m].rearrange("p k c -> p (k c)")).then_inc(s_w.h, 16)
                g.dma_start(out=qkw[:, 1024:2048],
                            in_=pb.dram["wk1at"][m].rearrange("p k c -> p (k c)")).then_inc(s_w.h, 16)
                s_w.n += 32

            qk_mm = {}
            for g_i in range(8):
                qk_mm[g_i] = s_mm.n + g_i + 1
            evq0 = s_ev.n
            prev_S_last = s_S.n

            @pb.sec("tensor")
            def _(t, base=base, qk_t=qk_t, qk_mm=qk_mm, evq0=evq0, v_done=v_done,
                  xq_t=s_x.n):
                t.wait_ge(s_w.h, qk_t)
                t.wait_ge(s_ev.h, v_done)
                t.wait_ge(s_x.h, xq_t)
                for g_i in range(8):
                    io_, cc = divmod(g_i, 4)
                    if g_i >= 2:
                        t.wait_ge(s_ev.h, evq0 + g_i - 1)
                    for k in range(ND):
                        ins = nc.tensor.matmul(pv[g_i % 2][:],
                                               qkw[:, io_ * 1024 + k * 128:io_ * 1024 + (k + 1) * 128],
                                               xnT[k][:, base + cc * 512:base + (cc + 1) * 512],
                                               start=(k == 0), stop=(k == ND - 1))
                    s_mm.inc(ins, 1)

            @pb.sec("vector")
            def _(v, m=m, qk_mm=qk_mm, prev_S_last=prev_S_last):
                v.wait_ge(s_S.h, prev_S_last)
                for g_i in range(8):
                    io_, cc = divmod(g_i, 4)
                    v.wait_ge(s_mm.h, qk_mm[g_i])
                    bcol = b1a[:, m:m + 1] if io_ == 0 else b1a[:, 8 + m:9 + m]
                    s_ev.inc(v.tensor_scalar(out=qkT[:, io_ * 2048 + cc * 512:io_ * 2048 + (cc + 1) * 512],
                                             in0=pv[g_i % 2][:], scalar1=bcol, scalar2=None,
                                             op0=ALU.add), 1)
            qk_done = s_ev.n

            for hh in range(2):
                h = 2 * m + hh
                hrow = hh * 64
                gp0 = s_gp.n
                ga0 = s_ga.n
                gt0 = s_gt.n
                bb_t = s_bb.n + 16

                @pb.sec("gpsimd")
                def _(g, h=h, gt0=gt0):
                    g.wait_ge(s_sb.h, s_sb.n)
                    g.dma_start(out=bbuf, in_=pb.dram["btab1"][h]).then_inc(s_bb.h, 16)
                    s_bb.n += 16

                o_base = s_o.n

                @pb.sec("tensor")
                def _(t, base=base, h=h, qk_done=qk_done, gp0=gp0, ga0=ga0, o_base=o_base):
                    t.wait_ge(s_ev.h, qk_done)
                    t.wait_ge(s_c.h, gt_base)
                    t.wait_ge(s_o.h, o_base)
                    for cc in range(4):
                        if cc >= 1:
                            t.wait_ge(s_ga.h, ga0 + cc)
                        wsum = auxf[0:64, 0:2] if h % 2 == 0 else auxf[64:128, 0:2]
                        ins = nc.tensor.matmul(prz[0:2, :], wsum,
                                                xnT[h // 2][64 * (h % 2):64 * (h % 2) + 64,
                                                            base + cc * 512:base + (cc + 1) * 512],
                                                start=True, stop=True)
                        s_gp.inc(ins, 1)

                @pb.sec("scalar")
                def _(sc, gp0=gp0, ga0=ga0):
                    for cc in range(4):
                        sc.wait_ge(s_gp.h, gp0 + cc + 1)
                        s_ga.inc(sc.activation(out=gatesT[:, cc * 512:(cc + 1) * 512],
                                               in_=prz[0:2, :], func=AF.Sigmoid,
                                               bias=sb["st1"][0:2, 0:1]), 1)

                s_gb = pb.sem("a1gb")
                gb_t = s_gb.n + 16

                @pb.sec("gpsimd")
                def _(g, ga0=ga0):
                    g.wait_ge(s_ga.h, ga0 + 4)
                    g.wait_ge(s_gt.h, s_gt.n)
                    s_gb.inc(g.dma_start(out=grow_f, in_=gatesT[1:2, :]), 16)



                @pb.sec("vector")
                def _(v, h=h, ga0=ga0, gt0=gt0, gb_t=gb_t):
                    v.wait_ge(s_ga.h, ga0 + 4)
                    v.wait_ge(s_gb.h, gb_t)
                    pb.vsync(v, v.tensor_scalar(out=grow_f, in0=grow_f,
                                                scalar1=g1bc[:, 8 + h:9 + h], scalar2=-1.0,
                                                op0=ALU.mult, op1=ALU.add))
                    pb.vsync(v, v.tensor_tensor(out=grow_f, in0=gatesT[0:1, :],
                                                in1=grow_f, op=ALU.mult))
                    pb.vsync(v, v.tensor_scalar(out=grow_f, in0=grow_f,
                                                scalar1=2.0, scalar2=8.0, op0=ALU.add,
                                                op1=ALU.mult))
                    s_gt.inc(v.tensor_copy(growb, grow_f), 1)

                S_t, sb_t, x_t, av_t, rz_t, rp_t, o_t, gr_t, gc_t = {}, {}, {}, {}, {}, {}, {}, {}, {}
                cS, csb, cx, cav, crz, crp, co, cgr, cgc = (s_S.n, s_sb.n, s_x.n, s_av.n,
                                                            s_rz.n, s_rp.n, s_o.n, s_gr.n, s_gc.n)
                sb_base, x_base, av_base = s_S.n, s_x.n, s_av.n
                for ic in range(4):
                    cgr += 1
                    gr_t[ic] = cgr
                    cgc += 1
                    gc_t[ic] = cgc
                    for jb in range(NJB):
                        cS += 1
                        S_t[(ic, jb)] = cS
                        csb += 1
                        sb_t[(ic, jb)] = csb
                        cx += 1
                        x_t[(ic, jb)] = cx
                        cav += 1
                        av_t[(ic, jb)] = cav
                    crz += 1
                    rz_t[ic] = crz
                    crp += 1
                    rp_t[ic] = crp
                    co += 1
                    o_t[ic] = co

                @pb.sec("tensor")
                def _(t, base=base, h=h, hrow=hrow, gt0=gt0, S_t=S_t, sb_t=sb_t,
                      x_t=x_t, av_t=av_t, rz_t=rz_t, rp_t=rp_t, o_t=o_t, gr_t=gr_t,
                      gc_t=gc_t, ga0=ga0):
                    LAG = 2
                    t.wait_ge(s_gt.h, gt0 + 1)
                    for ic in range(4):
                        if ic == 0:
                            t.wait_ge(s_ga.h, ga0 + 4)
                        else:
                            t.wait_ge(s_o.h, o_t[ic - 1])
                        if ic >= 1:
                            t.wait_ge(s_gc.h, gc_t[ic - 1])
                        s_gr.inc(nc.tensor.matmul(prz[:], sb["onesb"][0:1, :],
                                                  growb[:, ic * 512:(ic + 1) * 512],
                                                  start=True, stop=True), 1)
                        for step in range(NJB + LAG):
                            if step < NJB:
                                jb = step
                                pj = jb % 4
                                occ = (ic, jb - 4) if jb >= 4 else ((ic - 1, NJB - 4 + jb) if ic > 0 else None)
                                if occ in x_t:
                                    t.wait_ge(s_x.h, x_t[occ])
                                else:
                                    t.wait_ge(s_x.h, x_base)
                                nc.tensor.matmul(
                                    pS[pj][:], qkT[hrow:hrow + 64, 2048 + jb * 128:2048 + (jb + 1) * 128],
                                    qkT[hrow:hrow + 64, ic * 512:(ic + 1) * 512], start=True, stop=False)
                                t.wait_ge(s_sb.h, sb_t[(ic, jb)])
                                ins = nc.tensor.matmul(pS[pj][:], sb["ident"][:], t1sl[pj],
                                                       start=False, stop=True)
                                s_S.inc(ins, 1)
                            if step >= LAG:
                                jb2 = step - LAG
                                pj2 = jb2 % 4
                                t.wait_ge(s_x.h, x_t[(ic, jb2)])
                                ins = nc.tensor.matmul(pav[0:65, :],
                                                       vbuf[:, jb2 * 1040 + h * VS:jb2 * 1040 + (h + 1) * VS],
                                                       ptile[pj2], start=(jb2 == 0), stop=(jb2 == NJB - 1))
                                s_av.inc(ins, 1)
                        t.wait_ge(s_rz.h, rz_t[ic])
                        s_rp.inc(nc.tensor.matmul(prz[0:64, :], sb["onesb"][0:1, 0:64],
                                                  zrow_b, start=True, stop=True), 1)

                @pb.sec("vector")
                def _(v, h=h, hrow=hrow, S_t=S_t, sb_t=sb_t, x_t=x_t, av_t=av_t,
                      rz_t=rz_t, rp_t=rp_t, o_t=o_t, gr_t=gr_t, gc_t=gc_t,
                      st_prev=st_prev, m=m, bb_t=bb_t):
                    v.wait_ge(s_bb.h, bb_t)
                    for ic in range(4):
                        v.wait_ge(s_gr.h, gr_t[ic])
                        s_gc.inc(v.tensor_copy(grep, prz[:]), 1)
                        v.wait_ge(s_gc.h, gc_t[ic])
                        for jb in range(NJB):
                            pj = jb % 4
                            off = 1920 - 128 * jb + 512 * ic
                            occ = (ic, jb - 4) if jb >= 4 else ((ic - 1, NJB - 4 + jb) if ic > 0 else None)
                            if occ in S_t:
                                v.wait_ge(s_S.h, S_t[occ])
                            else:
                                v.wait_ge(s_S.h, sb_base)
                            s_sb.inc(v.tensor_tensor(out=t1sl[pj], in0=bbuf[:, off:off + 512],
                                                     in1=grep, op=ALU.mult), 1)
                        v.wait_ge(s_av.h, av_t[(ic, NJB - 1)])
                        pb.vsync(v, v.tensor_copy(zrow_f, pav[64:65, :]))
                        pb.vsync(v, v.reciprocal(zrow_f, zrow_f))
                        s_rz.inc(v.tensor_copy(zrow_b, zrow_f), 1)
                        v.wait_ge(s_rp.h, rp_t[ic])
                        pb.vsync(v, v.tensor_copy(zrep, prz[0:64, :]))
                        if st_prev > 0 and ic == 0 and hh == 0:
                            v.wait_ge(s_st.h, st_prev)
                        s_o.inc(v.tensor_tensor(out=oT[hrow:hrow + 64, ic * 512:(ic + 1) * 512],
                                                in0=pav[0:64, :], in1=zrep, op=ALU.mult), 1)

                @pb.sec("scalar")
                def _(sc, S_t=S_t, sb_t=sb_t, x_t=x_t, av_t=av_t):
                    for ic in range(4):
                        for jb in range(NJB):
                            pj = jb % 4
                            sc.wait_ge(s_S.h, S_t[(ic, jb)])
                            occ = (ic, jb - 4) if jb >= 4 else ((ic - 1, NJB - 4 + jb) if ic > 0 else None)
                            if occ in av_t:
                                sc.wait_ge(s_av.h, av_t[occ])
                            else:
                                sc.wait_ge(s_av.h, av_base)
                            s_x.inc(sc.activation(out=ptile[pj], in_=pS[pj][:], func=AF.Exp,
                                                  scale=0.125), 1)

            @pb.sec("sync")
            def _(s, base=base, m=m):
                s.wait_ge(s_o.h, s_o.n)
                s_st.inc(s.dma_start(out=pb.dram["acc_aT"][m * 128:(m + 1) * 128, base:base + S],
                                     in_=oT[:, 0:S]), 16)

    a_done = s_st.n
    wo_t = s_w.n + 16 * (ND + 1)
    s_hl = pb.sem("a1hl")

    @pb.sec("gpsimd")
    def _(g, wo_t=wo_t):
        g.wait_ge(s_mm.h, s_mm.n)
        g.wait_ge(s_o.h, s_o.n)
        for k in range(ND):
            g.dma_start(out=sb["wB"][:, k * 1024:(k + 1) * 1024],
                        in_=pb.dram["wo1t"][k * 128:(k + 1) * 128, :]).then_inc(s_w.h, 16)
        g.dma_start(out=sb["silg"][0:1, 0:1024], in_=pb.dram["bo1row"][:]).then_inc(s_w.h, 16)
        s_w.n += 16 * (ND + 1)

    for s_ in range(BPC):
        base = s_ * S
        hl_t = s_hl.n + 16 * ND

        @pb.sec("gpsimd")
        def _(g, base=base, hl_t=hl_t, a_done=a_done, s_=s_):
            g.wait_ge(s_st.h, a_done)
            if s_ > 0:
                g.wait_ge(s_mm.h, s_mm.n)
            for k in range(ND):
                g.dma_start(out=sb["hT"][:, k * 2048:(k + 1) * 2048],
                            in_=pb.dram["acc_aT"][k * 128:(k + 1) * 128, base:base + S]).then_inc(s_hl.h, 16)
            s_hl.n += 16 * ND

        ob_stB = [sb["ob0"][:], sb["ob1"][:]]
        s_stB = [pb.sem("a1sb_0"), pb.sem("a1sb_1")]
        og_t, oe_t, ost_t = {}, {}, {}
        cmm2, cev2 = s_mm.n, s_ev.n
        cst2 = [s_stB[0].n, s_stB[1].n]
        for tt in range(NTT):
            for n in range(2):
                cmm2 += 1
                og_t[(tt, n)] = cmm2
                cev2 += 1
                oe_t[(tt, n)] = cev2
            cst2[tt % 2] += 16
            ost_t[tt] = cst2[tt % 2]
        prev_oe_last = s_ev.n
        prev_stB = [s_stB[0].n, s_stB[1].n]

        @pb.sec("tensor")
        def _(t, hl_t=hl_t, og_t=og_t, oe_t=oe_t, wo_t=wo_t, prev_oe_last=prev_oe_last,
              xB_t=s_x.n):
            t.wait_ge(s_hl.h, hl_t)
            t.wait_ge(s_w.h, wo_t)
            t.wait_ge(s_x.h, xB_t)
            t.wait_ge(s_ev.h, prev_oe_last)
            for tt in range(NTT):
                for n in range(2):
                    g_i = tt * 2 + n
                    if g_i >= 2:
                        t.wait_ge(s_ev.h, oe_t[((g_i - 2) // 2, (g_i - 2) % 2)])
                    for k in range(ND):
                        nc.tensor.matmul(pv[g_i % 2][:],
                                         sb["hT"][:, k * 2048 + tt * 128:k * 2048 + (tt + 1) * 128],
                                         sb["wB"][:, k * 1024 + n * 512:k * 1024 + (n + 1) * 512],
                                         start=(k == 0), stop=False)
                    ins = nc.tensor.matmul(pv[g_i % 2][:], sb["onesb"][0:1, 0:128],
                                           sb["silg"][0:1, n * 512:(n + 1) * 512],
                                           start=False, stop=True)
                    s_mm.inc(ins, 1)

        @pb.sec("vector")
        def _(v, og_t=og_t, ost_t=ost_t, s_=s_, prev_stB=prev_stB):
            for tt in range(NTT):
                for n in range(2):
                    v.wait_ge(s_mm.h, og_t[(tt, n)])
                    if n == 0:
                        if tt >= 2:
                            v.wait_ge(s_stB[tt % 2].h, ost_t[tt - 2])
                        elif s_ > 0:
                            v.wait_ge(s_stB[tt % 2].h, prev_stB[tt % 2])
                    s_ev.inc(v.tensor_copy(ob_stB[tt % 2][:, n * 512:(n + 1) * 512],
                                           pv[(tt * 2 + n) % 2][:]), 1)

        @pb.sec("sync")
        def _(s, base=base, oe_t=oe_t, s_stB=s_stB):
            for tt in range(NTT):
                s.wait_ge(s_ev.h, oe_t[(tt, 1)])
                row = base + tt * 128
                s_stB[tt % 2].inc(s.dma_start(out=pb.dram["acc_a"][row:row + 128, :],
                                              in_=ob_stB[tt % 2][:]), 16)

    @pb.sec("sync")
    def _(s):
        for ss in ("a1sb_0", "a1sb_1"):
            s.wait_ge(pb.sems[ss].h, pb.sems[ss].n)

    pb.barrier()



def p_route0(pb):
    nc = pb.nc
    sb = pb.sbuf
    rvx = sb["rvecs"]
    allk = rvx[0:6, 0:2048]
    m_out = rvx[0:6, 2048:4096]
    m_hi = rvx[0:6, 4096:6144]
    scr2 = rvx[0:6, 6144:8192]
    tf = sb["gen1"][0:2, 0:2048]
    negb = sb["gen0"][0:2, 0:2048]
    est = sb["pt0"][0:2, 0:2048]
    st0 = sb["st0"]
    LOGK = float(np.log(min(1024 * 9.0 / 8.0, 2048.0)))

    s_l = pb.sem("r0l")
    s_v = pb.sem("r0v")
    s_a = pb.sem("r0a")
    s_d = pb.sem("r0d")
    s_k = pb.sem("r0k")

    @pb.sec("sync")
    def _(s):
        s_l.inc(s.dma_start(out=allk[0:4, 0:2048],
                            in_=pb.dram["rv"][0:2, :].rearrange("r (s n) -> (r s) n", n=S)), 16)
        s_l.inc(s.dma_start(out=tf, in_=pb.dram["rv"][2:3, :].rearrange("r (s n) -> (r s) n", n=S)), 16)
    l0 = s_l.n

    va, aa = s_v.n, s_a.n

    @pb.sec("scalar")
    def _(sc):
        for it in range(50):
            sc.wait_ge(s_v.h, va + 2 * it + 1)
            s_a.inc(sc.activation(out=est, in_=negb, func=AF.Exp,
                                  accum_out=st0[0:2, 0:1]), 1)
            sc.wait_ge(s_v.h, va + 2 * it + 2)
            s_a.inc(sc.activation(out=st0[0:2, 1:2], in_=st0[0:2, 0:1], func=AF.Ln), 1)

    @pb.sec("vector")
    def _(v):
        v.wait_ge(s_l.h, l0)
        pb.vsync(v, v.tensor_copy(negb, tf))
        for it in range(50):
            s_v.inc(v.tensor_tensor(out=negb, in0=tf, in1=negb, op=ALU.subtract), 1)
            v.wait_ge(s_a.h, aa + 2 * it + 1)
            s_v.inc(v.memset(st0[0:2, 15:16], 0.0), 1)
            v.wait_ge(s_a.h, aa + 2 * it + 2)
            pb.vsync(v, v.tensor_scalar(out=st0[0:2, 2:3], in0=st0[0:2, 1:2],
                                        scalar1=-1.0, scalar2=LOGK, op0=ALU.mult, op1=ALU.add))
            pb.vsync(v, v.tensor_scalar(out=negb, in0=tf, scalar1=st0[0:2, 2:3],
                                        scalar2=0.0, op0=ALU.add, op1=ALU.max))
        s_v.inc(v.tensor_scalar(out=tf, in0=tf, scalar1=st0[0:2, 2:3],
                                scalar2=0.0, op0=ALU.add, op1=ALU.min), 1)
    key_done = s_v.n

    @pb.sec("gpsimd")
    def _(g):
        g.wait_ge(s_v.h, key_done)
        s_k.inc(g.dma_start(out=allk[4:6, :], in_=tf), 16)
    kd = s_k.n

    @pb.sec("vector")
    def _(v):
        v.wait_ge(s_k.h, kd)
        lo, hi, mid, cnt = st0[0:6, 4:5], st0[0:6, 5:6], st0[0:6, 6:7], st0[0:6, 7:8]
        gcol = st0[0:6, 8:9]
        pb.vsync(v, v.memset(lo, -1200.0))
        pb.vsync(v, v.memset(hi, 1200.0))
        for it in range(45):
            pb.vsync(v, v.tensor_tensor(out=mid, in0=lo, in1=hi, op=ALU.add))
            pb.vsync(v, v.tensor_scalar(out=mid, in0=mid, scalar1=0.5, scalar2=None,
                                        op0=ALU.mult))
            pb.vsync(v, v.tensor_scalar(out=scr2, in0=allk, scalar1=mid, scalar2=None,
                                        op0=ALU.is_gt))
            pb.vsync(v, v.reduce_sum(cnt, scr2, axis=AX.X))
            pb.vsync(v, v.tensor_scalar(out=cnt, in0=cnt, scalar1=1024.0, scalar2=None,
                                        op0=ALU.is_ge))
            d1, d2 = st0[0:6, 9:10], st0[0:6, 10:11]
            pb.vsync(v, v.tensor_tensor(out=d1, in0=mid, in1=lo, op=ALU.subtract))
            pb.vsync(v, v.tensor_tensor(out=d2, in0=mid, in1=hi, op=ALU.subtract))
            pb.vsync(v, v.scalar_tensor_tensor(out=d1, in0=d1, scalar=cnt, in1=lo,
                                               op0=ALU.mult, op1=ALU.add))
            pb.vsync(v, v.tensor_copy(lo, d1))
            pb.vsync(v, v.tensor_scalar(out=cnt, in0=cnt, scalar1=-1.0, scalar2=1.0,
                                        op0=ALU.mult, op1=ALU.add))
            pb.vsync(v, v.scalar_tensor_tensor(out=d2, in0=d2, scalar=cnt, in1=hi,
                                               op0=ALU.mult, op1=ALU.add))
            pb.vsync(v, v.tensor_copy(hi, d2))
        pb.vsync(v, v.tensor_scalar(out=m_hi, in0=allk, scalar1=hi, scalar2=None,
                                    op0=ALU.is_gt))
        pb.vsync(v, v.reduce_sum(gcol, m_hi, axis=AX.X))
        pb.vsync(v, v.tensor_scalar(out=m_out, in0=allk, scalar1=lo, scalar2=None,
                                    op0=ALU.is_gt))
        pb.vsync(v, v.tensor_tensor(out=m_out, in0=m_out, in1=m_hi, op=ALU.subtract))
        pb.vsync(v, v.tensor_tensor_scan(out=scr2,
                                         data0=sb["small"][0:6, 0:1].broadcast_to([6, 2048]),
                                         data1=m_out, initial=0.0,
                                         op0=ALU.mult, op1=ALU.add))
        pb.vsync(v, v.tensor_scalar(out=scr2, in0=scr2, scalar1=gcol,
                                    scalar2=None, op0=ALU.add))
        pb.vsync(v, v.tensor_scalar(out=scr2, in0=scr2, scalar1=1024.0,
                                    scalar2=None, op0=ALU.is_le))
        pb.vsync(v, v.tensor_tensor(out=m_out, in0=m_out, in1=scr2, op=ALU.mult))
        s_v.inc(v.tensor_tensor(out=m_out, in0=m_out, in1=m_hi, op=ALU.add), 1)
    sel_done = s_v.n

    s_z = pb.sem("r0z")

    @pb.sec("vector")
    def _(v):
        s_z.inc(v.memset(sb["scores"][:], 0.0), 1)

    @pb.sec("sync")
    def _(s):
        s.wait_ge(s_z.h, s_z.n)
        s_d.inc(s.dma_start(out=pb.dram["rmcol"][:], in_=sb["scores"][:]), 16)
        s.wait_ge(s_v.h, sel_done)
        s_d.inc(s.dma_start(out=pb.dram["rmask"][0:3, :].rearrange("r (s n) -> (r s) n", n=S),
                            in_=m_out), 16)
        s.wait_ge(s_d.h, s_d.n)
        for slot, row, srcten in [(0, 0, "rmask"), (1, 1, "rmask"), (2, 2, "rmask"),
                                  (3, 3, "rv")]:
            for s2 in range(BPC):
                dst = pb.dram["rmcol"][:].rearrange("p (i r) -> p r i", r=8)[:, slot,
                                                                            s2 * 16:(s2 + 1) * 16]
                srcr = pb.dram[srcten][row:row + 1, s2 * S:(s2 + 1) * S].rearrange(
                    "o (i p) -> (o p) i", p=128)
                s_d.inc(s.dma_start(out=dst, in_=srcr, allow_slow_non_contiguous=True), 16)
        s.wait_ge(s_d.h, s_d.n)

    pb.barrier()



def p_trans0(pb):
    nc = pb.nc
    sb = pb.sbuf
    xlt = [sb["pt0"][:, 0:D], sb["pt1"][:, 0:D]]
    xnT = [sb[f"xnT{k}"] for k in range(ND)]
    ptr = [pb.psum["pb0"], pb.psum["pb1"]]
    lngb = sb["scores"][:, 48:64]

    s_l = [pb.sem("t0l_0"), pb.sem("t0l_1")]
    s_t = pb.sem("t0t")
    s_v = pb.sem("t0v")
    s_c = pb.sem("t0c")
    t0, v0 = s_t.n, s_v.n
    l0 = [s_l[0].n, s_l[1].n]

    @pb.sec("gpsimd")
    def _(g):
        s_c.inc(g.dma_start(out=lngb, in_=pb.dram["lngb0"][:]), 16)
        for i in range(NT):
            b = i % 2
            if i >= 2:
                g.wait_ge(s_t.h, t0 + i - 1)
            g.dma_start(out=xlt[b], in_=pb.dram["xl_tok"][i * 128:(i + 1) * 128, :]
                        ).then_inc(s_l[b].h, 16)
            s_l[b].n += 16

    @pb.sec("tensor")
    def _(t):
        for i in range(NT):
            b = i % 2
            t.wait_ge(s_l[b].h, l0[b] + 16 * (i // 2 + 1))
            if i >= 2:
                t.wait_ge(s_v.h, v0 + i - 1)
            for k in range(ND):
                ins = nc.tensor.transpose(ptr[b][:, k * 128:(k + 1) * 128],
                                          xlt[b][:, k * 128:(k + 1) * 128], sb["ident"][:])
            s_t.inc(ins, 1)

    @pb.sec("vector")
    def _(v):
        v.wait_ge(s_c.h, s_c.n)
        for i in range(NT):
            b = i % 2
            v.wait_ge(s_t.h, t0 + i + 1)
            for k in range(ND):
                ins = v.tensor_scalar(out=xnT[k][:, i * 128:(i + 1) * 128],
                                      in0=ptr[b][:, k * 128:(k + 1) * 128],
                                      scalar1=lngb[:, k:k + 1],
                                      scalar2=lngb[:, 8 + k:9 + k], op0=ALU.mult, op1=ALU.add)
            s_v.inc(ins, 1)

    pb.barrier()



def p_ffn0(pb):
    nc = pb.nc
    sb = pb.sbuf
    CH = 512
    NCH = T // CH
    xnT = [sb[f"xnT{k}"] for k in range(ND)]
    mv = [sb["mvA0"], sb["mvA1"]]
    rrep = sb["silg"][:, 0:512]
    rrow = sb["silg"][0:1, 512:1024]
    b0f = sb["gen1"][:, 1024:1088]
    b2l = sb["pt1"][0:1, 0:1024]
    b2h = sb["pt1"][0:1, 1024:2048]
    mcol = sb["scores"]
    ob_st = [sb["ob0"][:], sb["ob1"][:], sb["gen0"][:, 0:1024], sb["gen0"][:, 1024:2048]]
    pg = [pb.psum["pf0"], pb.psum["pf1"]]
    po = [pb.psum[f"pf{i}"] for i in range(2, 6)]
    prz = pb.psum["pf1"]

    s_w = pb.sem("f0w")
    s_w2 = [pb.sem("f0w2_0"), pb.sem("f0w2_1")]
    s_xl = [pb.sem("f0x_0"), pb.sem("f0x_1")]
    s_sr = pb.sem("f0sr")
    s_rp = pb.sem("f0rp")
    s_rc = pb.sem("f0rc")
    s_sc = pb.sem("f0sc")
    s_m1 = pb.sem("f0m1")
    s_g = pb.sem("f0g")
    s_m2 = pb.sem("f0m2")
    s_e = pb.sem("f0e")
    s_al = [pb.sem(f"f0al_{i_}") for i_ in range(4)]
    s_as = [pb.sem(f"f0as_{i_}") for i_ in range(4)]
    s_c = pb.sem("f0c")

    @pb.sec("gpsimd")
    def _(g):
        s_c.inc(g.dma_start(out=b0f, in_=pb.dram["b0f"][:]), 16)
        s_c.inc(g.dma_start(out=b2l, in_=pb.dram["b0r2"][0:1, :]), 16)
        s_c.inc(g.dma_start(out=b2h, in_=pb.dram["b0r2"][1:2, :]), 16)
        s_c.inc(g.dma_start(out=mcol[:], in_=pb.dram["rmcol"][:]), 16)
    c0 = s_c.n

    mscr = sb["pt0"][:].bitcast(F32)[:, 0:512]
    for part in range(2):
        if part == 1:
            m1_prev_last = s_m1.n
            rp_prev_last = s_rp.n
        M0 = 4 if part == 0 else 32
        NK2 = 4 if part == 0 else 32
        gcol0 = 40 if part == 0 else 48
        b1c0 = 0 if part == 0 else 8
        b2row = b2l if part == 0 else b2h
        wname = "w0l1t" if part == 0 else "w0h1t"
        w2name = "w0l2" if part == 0 else "w0h2"
        HCOLS = M0 * CH

        w1_t = s_w.n + 16 * ND

        @pb.sec("gpsimd")
        def _(g, part=part, wname=wname):
            g.wait_ge(s_c.h, c0)
            if part == 0:
                for k in range(ND):
                    g.dma_start(out=sb["wB"][:, k * 512:(k + 1) * 512],
                                in_=pb.dram[wname][k].rearrange("p m c -> p (m c)")
                                ).then_inc(s_w.h, 16)
            else:
                g.wait_ge(s_m1.h, s_m1.n)
                g.wait_ge(s_m2.h, s_m2.n)
                g.wait_ge(s_sc.h, s_sc.n)
                for k in range(ND):
                    g.dma_start(out=xnT[k][:],
                                in_=pb.dram[wname][k].rearrange("p m c -> p (m c)")
                                ).then_inc(s_w.h, 16)
            s_w.n += 16 * ND

        sr_t, rp_t, rc_t, sc_t, xl_t, m1_t, g_t, w2_t, m2_t, e_t, al_t, as_t = \
            {}, {}, {}, {}, {}, {}, {}, {}, {}, {}, {}, {}
        csr, crp, crc, csc = s_sr.n, s_rp.n, s_rc.n, s_sc.n
        cm1, cg, cm2, ce = s_m1.n, s_g.n, s_m2.n, s_e.n
        cxl = [s_xl[0].n, s_xl[1].n]
        cw2 = [s_w2[0].n, s_w2[1].n]
        cal = [x.n for x in s_al]
        cas = [x.n for x in s_as]
        for c in range(NCH):
            csr += 16
            sr_t[c] = csr
            crp += 1
            rp_t[c] = crp
            crc += 1
            rc_t[c] = crc
            if part == 1:
                cxl[c % 2] += 16 * ND
                xl_t[c] = cxl[c % 2]
            csc += 1
            sc_t[c] = csc
            for mm in range(M0):
                cm1 += 1
                m1_t[(c, mm)] = cm1
                cg += 1
                g_t[(c, mm)] = cg
            for n in range(2):
                for k2 in range(NK2):
                    cw2[k2 % 2] += 16
                    w2_t[(c, n, k2)] = cw2[k2 % 2]
                    cm2 += 1
                    m2_t[(c, n, k2)] = cm2
            for n in range(2):
                for mt in range(4):
                    ce += 1
                    e_t[(c, n, mt)] = ce
            for mt in range(4):
                if part == 1:
                    cal[mt] += 16
                    al_t[(c, mt)] = cal[mt]
                cas[mt] += 16
                as_t[(c, mt)] = cas[mt]

        @pb.sec("gpsimd")
        def _(g, part=part, sr_t=sr_t, rp_t=rp_t, w2_t=w2_t, m2_t=m2_t, w2name=w2name,
              NK2=NK2):
            for c in range(NCH):
                occ = c - 1
                if occ in rp_t:
                    g.wait_ge(s_rp.h, rp_t[occ])
                elif part == 1:
                    g.wait_ge(s_rp.h, rp_prev_last)
                g.dma_start(out=rrow, in_=pb.dram["rv"][3:4, c * 512:(c + 1) * 512]
                            ).then_inc(s_sr.h, 16)
                s_sr.n += 16
                for n in range(2):
                    for k2 in range(NK2):
                        pk = k2 % 2
                        occ2 = (c, n, k2 - 2)
                        if k2 < 2:
                            if n > 0:
                                occ2 = (c, 0, NK2 - 2 + k2)
                            elif c > 0:
                                occ2 = (c - 1, 1, NK2 - 2 + k2)
                            else:
                                occ2 = None
                        if occ2 in m2_t:
                            g.wait_ge(s_m2.h, m2_t[occ2])
                        elif c == 0 and k2 < 2 and s_m2.n > 0:
                            g.wait_ge(s_m2.h, s_m2.n)
                        g.dma_start(out=sb["wB"][:, 4096 + pk * 512:4096 + (pk + 1) * 512],
                                    in_=pb.dram[w2name][k2 * 128:(k2 + 1) * 128,
                                                        n * 512:(n + 1) * 512]).then_inc(s_w2[pk].h, 16)
                        s_w2[pk].n += 16

        @pb.sec("sync")
        def _(s, part=part, xl_t=xl_t, m1_t=m1_t, e_t=e_t, as_t=as_t, al_t=al_t,
              M0=M0):
            for c in range(NCH):
                pc = c % 2
                if part == 1:
                    if c >= 2:
                        s.wait_ge(s_m1.h, m1_t[(c - 2, M0 - 1)])
                    else:
                        s.wait_ge(s_m1.h, m1_prev_last)
                    for k in range(ND):
                        s.dma_start(out=mv[pc][:, k * 512:(k + 1) * 512],
                                    in_=pb.dram["xnT"][k * 128:(k + 1) * 128,
                                                       c * CH:(c + 1) * CH]).then_inc(s_xl[pc].h, 16)
                    s_xl[pc].n += 16 * ND
                for mt in range(4):
                    row = (c * 4 + mt) * 128
                    if part == 1:
                        pv_ = (c - 1, mt) if c > 0 else None
                        if pv_ in as_t:
                            s.wait_ge(s_as[mt].h, as_t[pv_])
                        else:
                            s.wait_ge(s_as[mt].h, as_t0[mt])
                        s.dma_start(out=ob_st[mt][:],
                                    in_=pb.dram["acc_f"][row:row + 128, :]).then_inc(s_al[mt].h, 16)
                        s_al[mt].n += 16
                for mt in range(4):
                    row = (c * 4 + mt) * 128
                    s.wait_ge(s_e.h, e_t[(c, 1, mt)])
                    s.dma_start(out=pb.dram["acc_f"][row:row + 128, :],
                                in_=ob_st[mt][:]).then_inc(s_as[mt].h, 16)
                    s_as[mt].n += 16
            if part == 1:
                for x in s_as:
                    s.wait_ge(x.h, x.n)
        if part == 0:
            as_t0 = dict((mt, as_t[(NCH - 1, mt)]) for mt in range(4))

        @pb.sec("vector")
        def _(v, part=part, rp_t=rp_t, rc_t=rc_t, sc_t=sc_t, xl_t=xl_t, m1_t=m1_t,
              m2_t=m2_t, e_t=e_t, al_t=al_t, as_t=as_t, gcol0=gcol0, M0=M0, NK2=NK2):
            for c in range(NCH):
                pc = c % 2
                v.wait_ge(s_rp.h, rp_t[c])
                s_rc.inc(v.tensor_copy(rrep, prz[:]), 1)
                v.wait_ge(s_rc.h, rc_t[c])
                if part == 1:
                    v.wait_ge(s_xl[pc].h, xl_t[c])
                if c >= 2:
                    v.wait_ge(s_m1.h, m1_t[(c - 2, M0 - 1)])
                elif part == 1:
                    v.wait_ge(s_m1.h, m1_prev_last)
                for k in range(ND):
                    src = xnT[k][:, c * CH:(c + 1) * CH] if part == 0 else mv[pc][:, k * 512:(k + 1) * 512]
                    ins = v.scalar_tensor_tensor(out=mv[pc][:, k * 512:(k + 1) * 512],
                                                 in0=src, scalar=b0f[:, gcol0 + k:gcol0 + k + 1],
                                                 in1=rrep, op0=ALU.mult, op1=ALU.mult)
                s_sc.inc(ins, 1)
                for n in range(2):
                    for mt in range(4):
                        v.wait_ge(s_m2.h, m2_t[(c, n, NK2 - 1)])
                        if part == 1:
                            if n == 0:
                                v.wait_ge(s_al[mt].h, al_t[(c, mt)])
                            pb.vsync(v, v.tensor_scalar(out=mscr,
                                                        in0=po[mt][:],
                                                        scalar1=mcol[:, (c * 4 + mt) * 8 + 2:(c * 4 + mt) * 8 + 3],
                                                        scalar2=None, op0=ALU.mult))
                            ins = v.tensor_tensor(out=ob_st[mt][:, n * 512:(n + 1) * 512],
                                                  in0=mscr,
                                                  in1=ob_st[mt][:, n * 512:(n + 1) * 512],
                                                  op=ALU.add)
                        else:
                            if n == 0:
                                pv_ = (c - 1, mt) if c > 0 else None
                                if pv_ in as_t:
                                    v.wait_ge(s_as[mt].h, as_t[pv_])
                            ins = v.tensor_copy(ob_st[mt][:, n * 512:(n + 1) * 512], po[mt][:])
                        s_e.inc(ins, 1)

        @pb.sec("tensor")
        def _(t, part=part, w1_t=w1_t, sr_t=sr_t, rp_t=rp_t, rc_t=rc_t, sc_t=sc_t,
              g_t=g_t, w2_t=w2_t, m2_t=m2_t, e_t=e_t, M0=M0, NK2=NK2, b2row=b2row,
              b1c0=b1c0):
            t.wait_ge(s_w.h, w1_t)
            for c in range(NCH):
                pc = c % 2
                t.wait_ge(s_sr.h, sr_t[c])
                occ = c - 1
                if occ in rc_t:
                    t.wait_ge(s_rc.h, rc_t[occ])
                if c > 0:
                    t.wait_ge(s_g.h, g_t[(c - 1, M0 - 1)])
                elif part == 1:
                    t.wait_ge(s_g.h, g_prev_last)
                s_rp.inc(nc.tensor.matmul(prz[:], sb["onesb"][0:1, :], rrow,
                                          start=True, stop=True), 1)
                t.wait_ge(s_rc.h, rc_t[c])
                t.wait_ge(s_sc.h, sc_t[c])
                for mm in range(M0):
                    pm_ = mm % 2
                    occ2 = (c, mm - 2)
                    if mm < 2:
                        occ2 = (c - 1, M0 - 2 + mm) if c > 0 else None
                    if occ2 in g_t:
                        t.wait_ge(s_g.h, g_t[occ2])
                    for k in range(ND):
                        lhs = (sb["wB"][:, k * 512 + mm * 128:k * 512 + (mm + 1) * 128]
                               if part == 0 else
                               xnT[k][:, mm * 128:(mm + 1) * 128])
                        ins = nc.tensor.matmul(pg[pm_][:], lhs,
                                               mv[pc][:, k * 512:(k + 1) * 512],
                                               start=(k == 0), stop=(k == ND - 1))
                    s_m1.inc(ins, 1)
                for n in range(2):
                    for k2 in range(NK2):
                        pk = k2 % 2
                        t.wait_ge(s_w2[pk].h, w2_t[(c, n, k2)])
                        if k2 == 0:
                            t.wait_ge(s_g.h, g_t[(c, M0 - 1)])
                            occp = (c, 0) if n > 0 else ((c - 1, 1) if c > 0 else None)
                            if occp is not None:
                                t.wait_ge(s_e.h, e_t[occp + (3,)])
                            elif s_e.n > 0 and part == 1:
                                t.wait_ge(s_e.h, e_t0_last)
                        for mt in range(4):
                            ins = nc.tensor.matmul(po[mt][:],
                                                   sb["hT"][:, k2 * 512 + mt * 128:k2 * 512 + (mt + 1) * 128],
                                                   sb["wB"][:, 4096 + pk * 512:4096 + (pk + 1) * 512],
                                                   start=(k2 == 0), stop=False)
                        if k2 == NK2 - 1:
                            for mt in range(4):
                                ins = nc.tensor.matmul(po[mt][:], sb["onesb"][0:1, 0:128],
                                                       b2row[:, n * 512:(n + 1) * 512],
                                                       start=False, stop=True)
                        s_m2.inc(ins, 1)
        if part == 0:
            e_t0_last = e_t[(NCH - 1, 1, 3)]
            g_prev_last = s_g.n + NCH * M0

        @pb.sec("scalar")
        def _(sc, part=part, m1_t=m1_t, g_t=g_t, m2_t=m2_t, M0=M0, NK2=NK2, b1c0=b1c0):
            for c in range(NCH):
                for mm in range(M0):
                    pm_ = mm % 2
                    sc.wait_ge(s_m1.h, m1_t[(c, mm)])
                    if c > 0:
                        sc.wait_ge(s_m2.h, m2_t[(c - 1, 1, NK2 - 1)])
                    elif part == 1:
                        sc.wait_ge(s_m2.h, s_m2base)
                    s_g.inc(sc.activation(out=sb["hT"][:, mm * 512:(mm + 1) * 512],
                                          in_=pg[pm_][:], func=AF.Gelu,
                                          bias=b0f[:, b1c0 + mm:b1c0 + mm + 1]), 1)
        if part == 0:
            s_m2base = s_m2.n

    pb.barrier()



def p_attn0(pb, part):
    nc = pb.nc
    sb = pb.sbuf
    NHH, HD, VS = 8, 64, 65
    NTT = S // 128
    xnT = [sb[f"xnT{k}"] for k in range(ND)]
    vbuf = sb["hT"]
    wvh = sb["hT"][:, 12544:16640]
    xs = [sb["gen0"][:].bitcast(BF16), sb["gen1"][:].bitcast(BF16)]
    qkT = sb["mvA0"]
    oT = sb["mvA1"]
    mqrow = sb["mvA1"][0:1, 2048:4096]
    zrow_f = sb["ob0"][0:1, 0:512]
    zrow_b = sb["silg"][0:1, 512:1024]
    zrep = sb["ob0"][64:128, 512:1024]
    mb = sb["xf0"][:, 0:32]
    gh0 = sb["xf0"][:, 32:40]
    mcol = sb["scores"]
    xfb = sb["xf0"][:].bitcast(BF16)
    qm = [xfb[:, 80:208], xfb[:, 208:336]]
    rfrep = sb["pt1"]
    rrow = sb["silg"][0:1, 0:512]
    ptile = [sb["pt0"][:, i_ * 512:(i_ + 1) * 512] for i_ in range(4)]
    pv = [pb.psum["pf0"], pb.psum["pf1"]]
    pS = [pb.psum["pf2"], pb.psum["pf3"], pb.psum["pf0"], pb.psum["pf1"]]
    pav = pb.psum["pf4"]
    prz = pb.psum["pf5"]

    s_w = pb.sem("a0w")
    s_c = pb.sem("a0c")
    s_sr = pb.sem("a0sr")
    s_rp = pb.sem("a0rp")
    s_rc = pb.sem("a0rc")
    s_sc = pb.sem("a0sc")
    s_mm = pb.sem("a0mm")
    s_ev = pb.sem("a0ev")
    s_S = pb.sem("a0S")
    s_x = pb.sem("a0x")
    s_pm = pb.sem("a0pm")
    s_av = pb.sem("a0av")
    s_rz = pb.sem("a0rz")
    s_rp2 = pb.sem("a0rp2")
    s_o = pb.sem("a0o")
    s_st = pb.sem("a0st")
    s_mq = pb.sem("a0mq")

    @pb.sec("gpsimd")
    def _(g):
        s_c.inc(g.dma_start(out=mcol[:], in_=pb.dram["rmcol"][:]), 16)
        s_c.inc(g.dma_start(out=gh0, in_=pb.dram["gh0"][:]), 16)
        s_c.inc(g.dma_start(out=qm[0], in_=pb.dram["qm"][:, 0:128]), 16)
        s_c.inc(g.dma_start(out=qm[1], in_=pb.dram["qm"][:, 128:256]), 16)
    c0 = s_c.n

    s_mb = pb.sem("a0mb")

    @pb.sec("vector")
    def _(v):
        v.wait_ge(s_c.h, c0)
        kvslice = mcol[:].rearrange("p (i r) -> p r i", r=8)[:, 1]
        pb.vsync(v, v.tensor_scalar(out=mb, in0=kvslice, scalar1=1e30, scalar2=-1e30,
                                    op0=ALU.mult, op1=ALU.add))
        s_mb.inc(v.memset(sb["dum"][:, 0:1], 0.0), 1)
    c1 = s_mb.n

    if True:
        wq_name = "wqh0t" if part == 0 else "wql0t"
        wk_name = "wkh0t" if part == 0 else "wkl0t"
        wv_name = "wvh0" if part == 0 else "wvl0"
        arow = 0 if part == 0 else 512

        wq_t = s_w.n + 16 * 2 * ND + 16 * ND

        @pb.sec("gpsimd")
        def _(g, part=part, wq_name=wq_name, wk_name=wk_name, wv_name=wv_name):
            if part == 1:
                g.wait_ge(s_mm.h, s_mm.n)
            for k in range(ND):
                g.dma_start(out=sb["wB"][:, k * 512:(k + 1) * 512],
                            in_=pb.dram[wq_name][k].rearrange("p m c -> p (m c)")).then_inc(s_w.h, 16)
                g.dma_start(out=sb["wB"][:, 4096 + k * 512:4096 + (k + 1) * 512],
                            in_=pb.dram[wk_name][k].rearrange("p m c -> p (m c)")).then_inc(s_w.h, 16)
                g.dma_start(out=wvh[:, k * 512:(k + 1) * 512],
                            in_=pb.dram[wv_name][k * 128:(k + 1) * 128, :]).then_inc(s_w.h, 16)
            s_w.n += 16 * 3 * ND

        s_gs = pb.sem("a0gs")
        gs_t = s_gs.n + (1 if part == 0 else 0)

        @pb.sec("vector")
        def _(v, part=part, wq_t=wq_t):
            if part == 0:
                v.wait_ge(s_w.h, wq_t)
                v.wait_ge(s_c.h, c0)
                for k in range(ND):
                    for reg in (sb["wB"][:, k * 512:(k + 1) * 512],
                                sb["wB"][:, 4096 + k * 512:4096 + (k + 1) * 512],
                                wvh[:, k * 512:(k + 1) * 512]):
                        ins = v.tensor_scalar(out=reg, in0=reg, scalar1=gh0[:, k:k + 1],
                                              scalar2=None, op0=ALU.mult)
                s_gs.inc(ins, 1)

        for s_ in range(BPC):
            base = s_ * S
            seq_t = s_ * 16

            mq_t = s_mq.n + 16

            @pb.sec("gpsimd")
            def _(g, part=part, base=base):
                if part == 0:
                    g.wait_ge(s_o.h, s_o.n)
                    g.dma_start(out=mqrow, in_=pb.dram["rmask"][0:1, base:base + S]
                                ).then_inc(s_mq.h, 16)
                    s_mq.n += 16

            vg_t, ev_t, sr_t, rp_t, rc_t = {}, {}, {}, {}, {}
            cvg, cev = s_mm.n, s_ev.n
            csr, crp, crc = s_sr.n, s_rp.n, s_rc.n
            av_prev = s_av.n
            for cc in range(4):
                if part == 0:
                    csr += 16
                    sr_t[cc] = csr
                    crp += 1
                    rp_t[cc] = crp
                    crc += 1
                    rc_t[cc] = crc
                for tt in range(4):
                    cvg += 1
                    vg_t[(cc, tt)] = cvg
                    cev += 1
                    ev_t[(cc, tt)] = cev

            @pb.sec("gpsimd")
            def _(g, part=part, base=base, rp_t=rp_t):
                if part == 0:
                    for cc in range(4):
                        occ = cc - 1
                        if occ in rp_t:
                            g.wait_ge(s_rp.h, rp_t[occ])
                        elif s_rp.n > 0:
                            g.wait_ge(s_rp.h, s_rp.n)
                        g.dma_start(out=rrow, in_=pb.dram["rv"][3:4, base + cc * 512:base + (cc + 1) * 512]
                                    ).then_inc(s_sr.h, 16)
                        s_sr.n += 16

            @pb.sec("tensor")
            def _(t, part=part, base=base, wq_t=wq_t, vg_t=vg_t, ev_t=ev_t,
                  sr_t=sr_t, rp_t=rp_t, rc_t=rc_t, gs_t=gs_t, xv_t=s_x.n):
                t.wait_ge(s_w.h, wq_t)
                t.wait_ge(s_mb.h, c1)
                t.wait_ge(s_x.h, xv_t)
                if part == 0:
                    t.wait_ge(s_gs.h, gs_t)
                for cc in range(4):
                    if part == 0:
                        t.wait_ge(s_sr.h, sr_t[cc])
                        occ = cc - 1
                        if occ in rc_t:
                            t.wait_ge(s_rc.h, rc_t[occ])
                        s_rp.inc(nc.tensor.matmul(prz[:], sb["onesb"][0:1, :], rrow,
                                                  start=True, stop=True), 1)
                    for tt in range(4):
                        g_i = cc * 4 + tt
                        if g_i >= 2:
                            t.wait_ge(s_ev.h, ev_t[((g_i - 2) // 4, (g_i - 2) % 4)])
                        for k in range(ND):
                            stat = xnT[k][:, base + cc * 512 + tt * 128:base + cc * 512 + (tt + 1) * 128]
                            ins = nc.tensor.matmul(pv[g_i % 2][:], stat,
                                                   wvh[:, k * 512:(k + 1) * 512],
                                                   start=(k == 0), stop=(k == ND - 1))
                        s_mm.inc(ins, 1)

            @pb.sec("vector")
            def _(v, part=part, base=base, seq_t=seq_t, vg_t=vg_t, rp_t=rp_t, rc_t=rc_t,
                  av_prev=av_prev):
                v.wait_ge(s_av.h, av_prev)
                for cc in range(4):
                    if part == 0:
                        v.wait_ge(s_rp.h, rp_t[cc])
                        s_rc.inc(v.tensor_copy(rfrep[:, cc * 512:(cc + 1) * 512], prz[:]), 1)
                    for tt in range(4):
                        g_i = cc * 4 + tt
                        v.wait_ge(s_mm.h, vg_t[(cc, tt)])
                        ttg = cc * 4 + tt
                        for h in range(NHH):
                            dst = vbuf[:, ttg * 520 + h * VS:ttg * 520 + h * VS + 64]
                            if part == 0:
                                ins = v.tensor_scalar(out=dst, in0=pv[g_i % 2][:, h * 64:(h + 1) * 64],
                                                      scalar1=mcol[:, (seq_t + ttg) * 8 + 3:(seq_t + ttg) * 8 + 4],
                                                      scalar2=None, op0=ALU.mult)
                            else:
                                ins = v.tensor_copy(dst, pv[g_i % 2][:, h * 64:(h + 1) * 64])
                        s_ev.inc(ins, 1)
                for ttg in range(16):
                    for h in range(NHH):
                        ins = v.memset(vbuf[:, ttg * 520 + h * VS + 64:ttg * 520 + h * VS + 65], 1.0)
                pb.vsync(v, ins)
            v_done = s_ev.n

            for m in range(4):
                qk_mm = {}
                base_mm = s_mm.n
                gi = 0
                for cc in range(4):
                    for io_ in range(2):
                        gi += 1
                        qk_mm[(cc, io_)] = base_mm + gi
                evq0 = s_ev.n
                prev_S_last = s_S.n

                @pb.sec("tensor")
                def _(t, part=part, base=base, m=m, qk_mm=qk_mm, evq0=evq0,
                      v_done=v_done, xq_t=s_x.n):
                    t.wait_ge(s_ev.h, v_done)
                    t.wait_ge(s_x.h, xq_t)
                    gi2 = 0
                    for cc in range(4):
                        for io_ in range(2):
                            gi2 += 1
                            if gi2 > 2:
                                t.wait_ge(s_ev.h, evq0 + gi2 - 2)
                            for k in range(ND):
                                mov = xnT[k][:, base + cc * 512:base + (cc + 1) * 512]
                                ins = nc.tensor.matmul(pv[gi2 % 2][:],
                                                       sb["wB"][:, io_ * 4096 + k * 512 + m * 128:
                                                                io_ * 4096 + k * 512 + (m + 1) * 128],
                                                       mov, start=(k == 0), stop=(k == ND - 1))
                            s_mm.inc(ins, 1)

                @pb.sec("vector")
                def _(v, part=part, qk_mm=qk_mm, prev_S_last=prev_S_last):
                    v.wait_ge(s_S.h, prev_S_last)
                    gi2 = 0
                    for cc in range(4):
                        for io_ in range(2):
                            gi2 += 1
                            v.wait_ge(s_mm.h, qk_mm[(cc, io_)])
                            dst = qkT[:, io_ * 2048 + cc * 512:io_ * 2048 + (cc + 1) * 512]
                            if part == 0:
                                ins = v.tensor_tensor(out=dst, in0=pv[gi2 % 2][:],
                                                      in1=rfrep[:, cc * 512:(cc + 1) * 512],
                                                      op=ALU.mult)
                            else:
                                ins = v.tensor_copy(dst, pv[gi2 % 2][:])
                            s_ev.inc(ins, 1)
                qk_done = s_ev.n

                for hh in range(2):
                    h = 2 * m + hh
                    hrow = hh * 64
                    S_t, x_t, pm_t, av_t, rz_t, rp2_t, o_t = {}, {}, {}, {}, {}, {}, {}
                    cS, cx, cpm, cav, crz, crp2, co = (s_S.n, s_x.n, s_pm.n, s_av.n,
                                                       s_rz.n, s_rp2.n, s_o.n)
                    sb_base, x_base, av_base = s_S.n, s_x.n, s_av.n
                    jbs = {}
                    for ic in range(4):
                        if part == 0:
                            jbl = list(range(16))
                        else:
                            jbl = [jb for jb in range(4 * ic - 1, 4 * ic + 5) if 0 <= jb < 16]
                        jbs[ic] = jbl
                        for jb in jbl:
                            cS += 1
                            S_t[(ic, jb)] = cS
                            cx += 1
                            x_t[(ic, jb)] = cx
                            if part == 1:
                                cpm += 1
                                pm_t[(ic, jb)] = cpm
                            cav += 1
                            av_t[(ic, jb)] = cav
                        crz += 1
                        rz_t[ic] = crz
                        crp2 += 1
                        rp2_t[ic] = crp2
                        co += 1
                        o_t[ic] = co

                    @pb.sec("tensor")
                    def _(t, part=part, h=h, hrow=hrow, seq_t=seq_t, qk_done=qk_done,
                          S_t=S_t, x_t=x_t, pm_t=pm_t, av_t=av_t, rz_t=rz_t,
                          rp2_t=rp2_t, o_t=o_t, jbs=jbs, x_base=x_base):
                        LAG = 2 if part == 0 else 3
                        t.wait_ge(s_ev.h, qk_done)
                        for ic in range(4):
                            jbl = jbs[ic]
                            nj = len(jbl)
                            if ic > 0:
                                t.wait_ge(s_o.h, o_t[ic - 1])
                            elif s_o.n > 0:
                                t.wait_ge(s_o.h, s_o.n)
                            for step in range(nj + LAG):
                                if step < nj:
                                    idx = step
                                    jb = jbl[idx]
                                    pj = idx % 4
                                    if idx >= 4:
                                        t.wait_ge(s_x.h, x_t[(ic, jbl[idx - 4])])
                                    elif ic == 0:
                                        t.wait_ge(s_x.h, x_base)
                                    ins = nc.tensor.matmul(
                                        pS[pj][:], qkT[hrow:hrow + 64, 2048 + jb * 128:2048 + (jb + 1) * 128],
                                        qkT[hrow:hrow + 64, ic * 512:(ic + 1) * 512], start=True, stop=True)
                                    s_S.inc(ins, 1)
                                if step >= LAG:
                                    idx2 = step - LAG
                                    jb2 = jbl[idx2]
                                    pj2 = idx2 % 4
                                    if part == 1:
                                        t.wait_ge(s_pm.h, pm_t[(ic, jb2)])
                                    else:
                                        t.wait_ge(s_x.h, x_t[(ic, jb2)])
                                    ins = nc.tensor.matmul(pav[0:65, :],
                                                           vbuf[:, jb2 * 520 + h * VS:jb2 * 520 + (h + 1) * VS],
                                                           ptile[pj2], start=(idx2 == 0),
                                                           stop=(idx2 == nj - 1))
                                    s_av.inc(ins, 1)
                            t.wait_ge(s_rz.h, rz_t[ic])
                            s_rp2.inc(nc.tensor.matmul(prz[0:64, :], sb["onesb"][0:1, 0:64],
                                                       zrow_b, start=True, stop=True), 1)

                    @pb.sec("scalar")
                    def _(sc, part=part, seq_t=seq_t, S_t=S_t, x_t=x_t, pm_t=pm_t,
                          av_t=av_t, jbs=jbs, av_base=av_base):
                        for ic in range(4):
                            jbl = jbs[ic]
                            for idx, jb in enumerate(jbl):
                                pj = idx % 4
                                sc.wait_ge(s_S.h, S_t[(ic, jb)])
                                occ = None
                                if idx >= 4:
                                    occ = (ic, jbl[idx - 4])
                                elif ic > 0:
                                    if part == 0:
                                        occ = (ic - 1, jbs[ic - 1][idx - 4])
                                    else:
                                        occ = (ic - 1, jbs[ic - 1][-1])
                                if occ in av_t:
                                    sc.wait_ge(s_av.h, av_t[occ])
                                else:
                                    sc.wait_ge(s_av.h, av_base)
                                if part == 0:
                                    ins = sc.activation(out=ptile[pj], in_=pS[pj][:],
                                                        func=AF.Exp, scale=0.125,
                                                        bias=mb[:, seq_t + jb:seq_t + jb + 1])
                                else:
                                    ins = sc.activation(out=ptile[pj], in_=pS[pj][:],
                                                        func=AF.Exp, scale=0.125)
                                s_x.inc(ins, 1)

                    @pb.sec("vector")
                    def _(v, part=part, h=h, hrow=hrow, base=base, S_t=S_t, x_t=x_t,
                          pm_t=pm_t, av_t=av_t, rz_t=rz_t, rp2_t=rp2_t, o_t=o_t,
                          jbs=jbs, m=m):
                        for ic in range(4):
                            jbl = jbs[ic]
                            for idx, jb in enumerate(jbl):
                                if part == 0:
                                    continue
                                pj = idx % 4
                                v.wait_ge(s_x.h, x_t[(ic, jb)])
                                ins = None
                                for isub in range(4):
                                    dlt = jb - (4 * ic + isub)
                                    col = isub * 128
                                    if abs(dlt) > 1:
                                        ins = v.memset(ptile[pj][:, col:col + 128], 0.0)
                                    elif dlt == -1:
                                        ins = v.tensor_tensor(out=ptile[pj][:, col:col + 128],
                                                              in0=ptile[pj][:, col:col + 128],
                                                              in1=qm[0], op=ALU.mult)
                                    elif dlt == 1:
                                        ins = v.tensor_tensor(out=ptile[pj][:, col:col + 128],
                                                              in0=ptile[pj][:, col:col + 128],
                                                              in1=qm[1], op=ALU.mult)
                                s_pm.inc(ins, 1)
                            v.wait_ge(s_av.h, av_t[(ic, jbl[-1])])
                            pb.vsync(v, v.tensor_copy(zrow_f, pav[64:65, :]))
                            if part == 0:
                                v.wait_ge(s_mq.h, mq_t)
                                pb.vsync(v, v.reciprocal(zrow_f, zrow_f))
                                pb.vsync(v, v.tensor_tensor(out=zrow_f, in0=zrow_f,
                                                            in1=mqrow[:, ic * 512:(ic + 1) * 512],
                                                            op=ALU.mult))
                            else:
                                pb.vsync(v, v.reciprocal(zrow_f, zrow_f))
                            s_rz.inc(v.tensor_copy(zrow_b, zrow_f), 1)
                            v.wait_ge(s_rp2.h, rp2_t[ic])
                            pb.vsync(v, v.tensor_copy(zrep, prz[0:64, :]))
                            if s_st.n > 0 and ic == 0 and hh == 0:
                                v.wait_ge(s_st.h, s_st.n)
                            s_o.inc(v.tensor_tensor(out=oT[hrow:hrow + 64, ic * 512:(ic + 1) * 512],
                                                    in0=pav[0:64, :], in1=zrep, op=ALU.mult), 1)

                @pb.sec("sync")
                def _(s, part=part, base=base, m=m, arow=arow):
                    s.wait_ge(s_o.h, s_o.n)
                    s_st.inc(s.dma_start(out=pb.dram["acc_aT"][arow + m * 128:arow + (m + 1) * 128,
                                                               base:base + S],
                                         in_=oT[:, 0:S]), 16)


    pb.barrier()


def p_attn0B(pb):
    nc = pb.nc
    sb = pb.sbuf
    NTT = S // 128
    pv = [pb.psum["pf0"], pb.psum["pf1"]]
    s_w = pb.sem("a0w")
    s_mm = pb.sem("a0mm")
    s_ev = pb.sem("a0ev")
    s_o = pb.sem("a0o")
    s_st = pb.sem("a0st")

    a_done = s_st.n
    wo_t = s_w.n + 16 * ND
    s_hl = pb.sem("a0hl")

    @pb.sec("gpsimd")
    def _(g, wo_t=wo_t):
        g.wait_ge(s_mm.h, s_mm.n)
        g.wait_ge(s_o.h, s_o.n)
        for k in range(ND):
            g.dma_start(out=sb["wB"][:, k * 1024:(k + 1) * 1024],
                        in_=pb.dram["wcat0"][k * 128:(k + 1) * 128, :]).then_inc(s_w.h, 16)
        s_w.n += 16 * ND

    for s_ in range(BPC):
        base = s_ * S
        hl_t = s_hl.n + 16 * ND

        @pb.sec("gpsimd")
        def _(g, base=base, hl_t=hl_t, a_done=a_done, s_=s_):
            g.wait_ge(s_st.h, a_done)
            if s_ > 0:
                g.wait_ge(s_mm.h, s_mm.n)
            for k in range(ND):
                g.dma_start(out=sb["hT"][:, k * 2048:(k + 1) * 2048],
                            in_=pb.dram["acc_aT"][k * 128:(k + 1) * 128, base:base + S]).then_inc(s_hl.h, 16)
            s_hl.n += 16 * ND

        ob_stB = [sb["ob0"][:], sb["ob1"][:]]
        s_stB = [pb.sem("a0sb_0"), pb.sem("a0sb_1")]
        og_t, oe_t, ost_t = {}, {}, {}
        cmm2, cev2 = s_mm.n, s_ev.n
        cst2 = [s_stB[0].n, s_stB[1].n]
        for tt in range(NTT):
            for n in range(2):
                cmm2 += 1
                og_t[(tt, n)] = cmm2
                cev2 += 1
                oe_t[(tt, n)] = cev2
            cst2[tt % 2] += 16
            ost_t[tt] = cst2[tt % 2]
        prev_oe_last = s_ev.n
        prev_stB = [s_stB[0].n, s_stB[1].n]

        @pb.sec("tensor")
        def _(t, hl_t=hl_t, og_t=og_t, oe_t=oe_t, wo_t=wo_t, prev_oe_last=prev_oe_last):
            t.wait_ge(s_hl.h, hl_t)
            t.wait_ge(s_w.h, wo_t)
            t.wait_ge(s_ev.h, prev_oe_last)
            for tt in range(NTT):
                for n in range(2):
                    g_i = tt * 2 + n
                    if g_i >= 2:
                        t.wait_ge(s_ev.h, oe_t[((g_i - 2) // 2, (g_i - 2) % 2)])
                    for k in range(ND):
                        ins = nc.tensor.matmul(pv[g_i % 2][:],
                                               sb["hT"][:, k * 2048 + tt * 128:k * 2048 + (tt + 1) * 128],
                                               sb["wB"][:, k * 1024 + n * 512:k * 1024 + (n + 1) * 512],
                                               start=(k == 0), stop=(k == ND - 1))
                    s_mm.inc(ins, 1)

        @pb.sec("vector")
        def _(v, og_t=og_t, ost_t=ost_t, s_=s_, prev_stB=prev_stB):
            for tt in range(NTT):
                for n in range(2):
                    v.wait_ge(s_mm.h, og_t[(tt, n)])
                    if n == 0:
                        if tt >= 2:
                            v.wait_ge(s_stB[tt % 2].h, ost_t[tt - 2])
                        elif s_ > 0:
                            v.wait_ge(s_stB[tt % 2].h, prev_stB[tt % 2])
                    s_ev.inc(v.tensor_copy(ob_stB[tt % 2][:, n * 512:(n + 1) * 512],
                                           pv[(tt * 2 + n) % 2][:]), 1)

        @pb.sec("sync")
        def _(s, base=base, oe_t=oe_t, s_stB=s_stB):
            for tt in range(NTT):
                s.wait_ge(s_ev.h, oe_t[(tt, 1)])
                row = base + tt * 128
                s_stB[tt % 2].inc(s.dma_start(out=pb.dram["acc_a"][row:row + 128, :],
                                              in_=ob_stB[tt % 2][:]), 16)

    @pb.sec("sync")
    def _(s):
        for ss in ("a0sb_0", "a0sb_1"):
            s.wait_ge(pb.sems[ss].h, pb.sems[ss].n)

    pb.barrier()


def declare_io(pb):
    pb.dram_t("x0", [T, D], F32, kind="ExternalInput")
    pb.dram_t("y", [T, D], F32, kind="ExternalOutput")
    pb.dram_t("identity", [128, 128], F32, kind="ExternalInput")
    pb.dram_t("onesf", [128, 128], F32, kind="ExternalInput")
    pb.dram_t("rvl0", [128, 3 * D], F32, kind="ExternalInput")
    pb.dram_t("wg1t", [24, 128, 8, 128], I32, kind="ExternalInput")
    pb.dram_t("wu1t", [24, 128, 8, 128], I32, kind="ExternalInput")
    pb.dram_t("wd1t", [3072, 1024], I32, kind="ExternalInput")
    pb.dram_t("wqkv3t", [24, 128, 8, 128], F32, kind="ExternalInput")
    pb.dram_t("wgate3t", [8, 128, 8, 128], F32, kind="ExternalInput")
    pb.dram_t("b3qkv", [128, 24], F32, kind="ExternalInput")
    pb.dram_t("b3g", [128, 8], F32, kind="ExternalInput")
    pb.dram_t("wqkv2t", [16, 128, 8, 128], F32, kind="ExternalInput")
    pb.dram_t("wv2t", [1024, 1024], F32, kind="ExternalInput")
    pb.dram_t("wo2t", [1024, 1024], F32, kind="ExternalInput")
    pb.dram_t("trimask", [128, 128], F32, kind="ExternalInput")
    pb.dram_t("w21t", [8, 8, 128, 32, 128], F32, kind="ExternalInput")
    pb.dram_t("w22t", [8, 4096, 1024], F32, kind="ExternalInput")
    pb.dram_t("b21", [8, 128, 32], F32, kind="ExternalInput")
    pb.dram_t("b22", [8, 1024], F32, kind="ExternalInput")
    pb.dram_t("wg3f", [8, 8, 128, 4, 128], F32, kind="ExternalInput")
    pb.dram_t("wu3f", [8, 8, 128, 4, 128], F32, kind="ExternalInput")
    pb.dram_t("wd3f", [8, 512, 1024], F32, kind="ExternalInput")
    pb.dram_t("gscr", [128, 4], F32)
    pb.dram_t("wq1at", [8, 128, 8, 128], F32, kind="ExternalInput")
    pb.dram_t("wk1at", [8, 128, 8, 128], F32, kind="ExternalInput")
    pb.dram_t("wv1t", [1024, 1024], F32, kind="ExternalInput")
    pb.dram_t("wo1t", [1024, 1024], F32, kind="ExternalInput")
    pb.dram_t("b1a", [128, 16], F32, kind="ExternalInput")
    pb.dram_t("bv1row", [1, 1024], F32, kind="ExternalInput")
    pb.dram_t("bo1row", [1, 1024], F32, kind="ExternalInput")
    pb.dram_t("g1W", [64, 8], F32, kind="ExternalInput")
    pb.dram_t("g1bc", [1, 24], F32, kind="ExternalInput")
    pb.dram_t("btab1", [16, 128, 3969], F32, kind="ExternalInput")
    pb.dram_t("lngb0", [128, 16], F32, kind="ExternalInput")
    pb.dram_t("w0l1t", [8, 128, 4, 128], F32, kind="ExternalInput")
    pb.dram_t("w0l2", [512, 1024], F32, kind="ExternalInput")
    pb.dram_t("w0h1t", [8, 128, 32, 128], F32, kind="ExternalInput")
    pb.dram_t("w0h2", [4096, 1024], F32, kind="ExternalInput")
    pb.dram_t("b0f", [128, 64], F32, kind="ExternalInput")
    pb.dram_t("b0r2", [2, 1024], F32, kind="ExternalInput")
    pb.dram_t("wqh0t", [8, 128, 4, 128], F32, kind="ExternalInput")
    pb.dram_t("wkh0t", [8, 128, 4, 128], F32, kind="ExternalInput")
    pb.dram_t("wql0t", [8, 128, 4, 128], F32, kind="ExternalInput")
    pb.dram_t("wkl0t", [8, 128, 4, 128], F32, kind="ExternalInput")
    pb.dram_t("wvh0", [1024, 512], F32, kind="ExternalInput")
    pb.dram_t("wvl0", [1024, 512], F32, kind="ExternalInput")
    pb.dram_t("wcat0", [1024, 1024], F32, kind="ExternalInput")
    pb.dram_t("gh0", [128, 8], F32, kind="ExternalInput")
    pb.dram_t("qm", [128, 256], F32, kind="ExternalInput")
    pb.dram_t("rvl2", [128, 8 * D], F32, kind="ExternalInput")
    pb.dram_t("rvl3", [128, 8 * D], F32, kind="ExternalInput")



def host_inputs(inputs, core):
    f = np.ascontiguousarray
    m = {}
    x = inputs["L0_attn_x"]
    m["x0"] = f(x[core * BPC:(core + 1) * BPC].reshape(T, D))
    m["identity"] = np.eye(128, dtype=np.float32)
    m["onesf"] = np.ones((128, 128), dtype=np.float32)
    rv0 = np.stack([inputs["L0_attn_rt_q"], inputs["L0_attn_rt_kv"],
                    inputs["L0_ffn_routing_token"]], 0)
    m["rvl0"] = f(np.broadcast_to(rv0[None, :, :], (128, 3, D)).reshape(128, 3 * D))
    rv2 = inputs["L2_ffn_w_router"].T
    m["rvl2"] = f(np.broadcast_to(rv2[None, :, :], (128, 8, D)).reshape(128, 8 * D))
    rv3 = inputs["L3_ffn_gate_w"]
    m["rvl3"] = f(np.broadcast_to(rv3[None, :, :], (128, 8, D)).reshape(128, 8 * D))
    def tile_mkpc(w):
        wt = w.T
        return f(wt.reshape(8, 128, 24, 128).transpose(2, 1, 0, 3))
    m["wg1t"] = tile_mkpc(inputs["L1_ffn_w_gate"])
    m["wu1t"] = tile_mkpc(inputs["L1_ffn_w_up"])
    m["wd1t"] = f(inputs["L1_ffn_w_down"].T)
    def tile_lhsT(wt, mtiles):
        return f(wt.reshape(8, 128, mtiles, 128).transpose(2, 1, 0, 3))
    m["wqkv3t"] = tile_lhsT(inputs["L3_attn_W_qkv"].T, 24)
    m["wgate3t"] = tile_lhsT(inputs["L3_attn_W_gate"].T, 8)
    m["b3qkv"] = f(inputs["L3_attn_b_qkv"].reshape(24, 128).T)
    m["b3g"] = f(inputs["L3_attn_b_gate"].reshape(8, 128).T)
    wqkvT = inputs["L2_attn_w_qkv"].T
    m["wqkv2t"] = f(wqkvT[:, 0:2048].reshape(8, 128, 16, 128).transpose(2, 1, 0, 3))
    m["wv2t"] = f(wqkvT[:, 2048:3072])
    m["wo2t"] = f(inputs["L2_attn_w_out"].T)
    m["trimask"] = f(np.triu(np.ones((128, 128), dtype=np.float32)))
    m["w21t"] = f(inputs["L2_ffn_w1"].reshape(8, 8, 128, 32, 128))
    m["w22t"] = f(inputs["L2_ffn_w2"])
    m["b21"] = f(inputs["L2_ffn_b1"].reshape(8, 32, 128).transpose(0, 2, 1))
    m["b22"] = f(inputs["L2_ffn_b2"])
    def tile_e(w):
        return f(np.stack([w[e].T.reshape(8, 128, 4, 128) for e in range(8)]))
    m["wg3f"] = tile_e(inputs["L3_ffn_w_gate"])
    m["wu3f"] = tile_e(inputs["L3_ffn_w_up"])
    m["wd3f"] = f(np.stack([inputs["L3_ffn_w_down"][e].T for e in range(8)]))
    def tile8(wt):
        return f(wt.reshape(8, 128, 8, 128).transpose(2, 1, 0, 3))
    m["wq1at"] = tile8(inputs["L1_attn_Wq"].T)
    m["wk1at"] = tile8(inputs["L1_attn_Wk"].T)
    m["wv1t"] = f(inputs["L1_attn_Wv"].T)
    m["wo1t"] = f(inputs["L1_attn_Wo"].T)
    m["b1a"] = f(np.concatenate([inputs["L1_attn_bq"], inputs["L1_attn_bk"]]).reshape(16, 128).T)
    m["bv1row"] = f(inputs["L1_attn_bv"].reshape(1, 1024))
    m["bo1row"] = f(inputs["L1_attn_bo"].reshape(1, 1024))
    m["g1W"] = f(inputs["L1_attn_gru_W"].T)
    m["g1bc"] = f(np.concatenate([inputs["L1_attn_gru_b"],
                                  inputs["L1_attn_gru_const"].reshape(16)]).reshape(1, 24))
    import math
    N = S
    ctx = np.arange(N)[:, None]
    rel = np.arange(-(N - 1), N)
    nb = 160
    buckets = (rel > 0).astype(np.int64) * nb
    ra = np.abs(rel)
    me = nb // 2
    small_ = ra < me
    large = np.log(np.maximum(ra, 1).astype(np.float32) / me) / math.log(800 / me) * (nb - me)
    large = (me + large).astype(np.int64)
    large = np.minimum(large, nb - 1)
    buck = buckets + np.where(small_, ra, large)
    table = inputs["L1_attn_rel_embed"][buck]
    rtable = table[::-1].T
    rtable = np.concatenate([rtable, np.zeros((16, 1), np.float32)], 1)
    idx = (127 - np.arange(128))[:, None] + np.arange(3969)[None, :]
    m["btab1"] = f(rtable[:, idx])
    m["lngb0"] = f(np.concatenate([inputs["L0_attn_ln_g"].reshape(8, 128),
                                   inputs["L0_attn_ln_b"].reshape(8, 128)]).T)
    m["w0l1t"] = f(inputs["L0_ffn_light_w1"].reshape(8, 128, 4, 128))
    m["w0l2"] = f(inputs["L0_ffn_light_w2"])
    m["w0h1t"] = f(inputs["L0_ffn_heavy_w1"].reshape(8, 128, 32, 128))
    m["w0h2"] = f(inputs["L0_ffn_heavy_w2"])
    b0f = np.zeros((128, 64), np.float32)
    b0f[:, 0:4] = inputs["L0_ffn_light_b1"].reshape(4, 128).T
    b0f[:, 8:40] = inputs["L0_ffn_heavy_b1"].reshape(32, 128).T
    b0f[:, 40:48] = inputs["L0_ffn_light_gamma"].reshape(8, 128).T
    b0f[:, 48:56] = inputs["L0_ffn_heavy_gamma"].reshape(8, 128).T
    m["b0f"] = f(b0f)
    m["b0r2"] = f(np.stack([inputs["L0_ffn_light_b2"], inputs["L0_ffn_heavy_b2"]]))
    def t4(w):
        return f(w.reshape(8, 128, 4, 128))
    m["wqh0t"] = t4(inputs["L0_attn_wq_h"])
    m["wkh0t"] = t4(inputs["L0_attn_wkv_h"][:, 0:512])
    m["wvh0"] = f(inputs["L0_attn_wkv_h"][:, 512:1024])
    m["wql0t"] = t4(inputs["L0_attn_wqkv_l"][:, 0:512])
    m["wkl0t"] = t4(inputs["L0_attn_wqkv_l"][:, 512:1024])
    m["wvl0"] = f(inputs["L0_attn_wqkv_l"][:, 1024:1536])
    m["wcat0"] = f(np.concatenate([inputs["L0_attn_wo_h"], inputs["L0_attn_wo_l"]], 0))
    m["gh0"] = f(inputs["L0_attn_gamma_h"].reshape(8, 128).T)
    qmv = np.zeros((128, 256), np.float32)
    jj = np.arange(128)[:, None]
    ii = np.arange(128)[None, :]
    qmv[:, 0:128] = ((jj >= 64) & (ii < 64)).astype(np.float32)
    qmv[:, 128:256] = ((jj < 64) & (ii >= 64)).astype(np.float32)
    m["qm"] = f(qmv)
    return m


LAYER_PHASES = {
    0: lambda pb, acc_in: (p0_ln(pb, 0), p_route0(pb), p_attn0(pb, 0), p_ffn0(pb),
                           p_trans0(pb), p_attn0(pb, 1), p_attn0B(pb)),
    1: lambda pb, acc_in: (p0_ln(pb, 1, acc_in=acc_in), p_attn1(pb), p_ffn1(pb)),
    2: lambda pb, acc_in: (p0_ln(pb, 2, acc_in=acc_in), p_attn2(pb), p_ffn2(pb)),
    3: lambda pb, acc_in: (p0_ln(pb, 3, acc_in=acc_in), p_ffn3(pb), p_attn3(pb),
                           p_final(pb)),
}


def build_layer(layer):
    dbg = {}
    if layer == 0:
        dbg = {"acc_a": "out", "acc_f": "out"}
        acc_in = ("acc_a", "acc_f")
    elif layer == 3:
        dbg = {"xcur2": "in", "acc_a_in": "in", "acc_f_in": "in"}
        acc_in = ("acc_a_in", "acc_f_in")
    else:
        dbg = {("x0" if layer == 1 else "xcur"): "in",
               "acc_a_in": "in", "acc_f_in": "in",
               ("xcur" if layer == 1 else "xcur2"): "out",
               "acc_a": "out", "acc_f": "out"}
        acc_in = ("acc_a_in", "acc_f_in")
    pb = PB(dbg=dbg)
    pb.dram_t("acc_a_in", [T, D], F32, kind="Internal")
    pb.dram_t("acc_f_in", [T, D], F32, kind="Internal")
    alloc_common(pb)
    declare_io(pb)
    pb.start()
    init_consts(pb)
    LAYER_PHASES[layer](pb, acc_in)
    pb.finish()
    return pb


def build_full():
    pb = PB()
    alloc_common(pb)
    declare_io(pb)
    pb.start()
    init_consts(pb)
    for layer in range(4):
        LAYER_PHASES[layer](pb, ("acc_a", "acc_f"))
    pb.finish()
    return pb


FUSED = True
_cache = {}


def kernel(**inputs):
    per_core = [host_inputs(inputs, c) for c in range(NCORES)]
    if FUSED:
        if "full" not in _cache:
            _cache["full"] = build_full()
        pb = _cache["full"]
        in_maps = [{n: m[n] for n in m if n in pb.dram} for m in per_core]
        res = run_bass_kernel_spmd(pb.nc, in_maps, core_ids=list(range(NCORES)))
        outs = [r["y"] for r in res.results]
    else:
        hand = [dict() for _ in range(NCORES)]
        for layer in range(4):
            if layer not in _cache:
                _cache[layer] = build_layer(layer)
            pb = _cache[layer]
            in_maps = []
            for c in range(NCORES):
                im = {}
                for name, kind in pb.kinds.items():
                    if kind != "ExternalInput":
                        continue
                    if name in hand[c]:
                        im[name] = hand[c][name]
                    elif name in per_core[c]:
                        im[name] = per_core[c][name]
                    else:
                        raise KeyError(f"layer {layer} missing input {name}")
                in_maps.append(im)
            res = run_bass_kernel_spmd(pb.nc, in_maps, core_ids=list(range(NCORES)))
            for c in range(NCORES):
                r = res.results[c]
                if layer == 0:
                    hand[c] = {"acc_a_in": r["acc_a"], "acc_f_in": r["acc_f"]}
                elif layer == 1:
                    hand[c] = {"acc_a_in": r["acc_a"], "acc_f_in": r["acc_f"],
                               "xcur": r["xcur"]}
                elif layer == 2:
                    hand[c] = {"acc_a_in": r["acc_a"], "acc_f_in": r["acc_f"],
                               "xcur2": r["xcur2"]}
                else:
                    hand[c] = {"y": r["y"]}
        outs = [hand[c]["y"] for c in range(NCORES)]
    y = np.stack([o.reshape(BPC, S, D) for o in outs]).reshape(B, S, D)
    return np.ascontiguousarray(y.astype(np.float32))
